# version 09
# baseline (speedup 1.0000x reference)
import numpy as np
import ml_dtypes
from contextlib import ExitStack

import concourse.bass as bass
import concourse.mybir as mybir
from concourse.bass_utils import run_bass_kernel_spmd

F32 = mybir.dt.float32
BF16 = mybir.dt.bfloat16
AF = mybir.ActivationFunctionType
ALU = mybir.AluOpType

NC = 8
B = 4
L = 8192
D = 1024
N = 16
R = 64
TK = L // NC
DK = D // NC
CH = 1024
NCH = L // CH
KB = D // 128


class Sem:

    def __init__(self, nc, name):
        self.h = nc.alloc_semaphore(name)
        self.count = 0

    def reserve(self, k=1):
        self.count += k
        return self.count

    def tag(self, instr, k=1):
        instr.then_inc(self.h, k)


STAGE = 3
DUMP = 0
SUB = 3
SUBY = 3


def build_program():
    nc = bass.Bass("TRN2", target_bir_lowering=False, debug=False, num_devices=NC)

    x_in = nc.dram_tensor("x_in", [B, TK, D], F32, kind="ExternalInput").ap()
    wx_in = nc.dram_tensor("wx_in", [D, 96], F32, kind="ExternalInput").ap()
    wdt_in = nc.dram_tensor("wdt_in", [R, DK], F32, kind="ExternalInput").ap()
    bdt_in = nc.dram_tensor("bdt_in", [DK, 1], F32, kind="ExternalInput").ap()
    alog_in = nc.dram_tensor("alog_in", [DK, N], F32, kind="ExternalInput").ap()
    dvec_in = nc.dram_tensor("dvec_in", [DK, 1], F32, kind="ExternalInput").ap()
    wbig_in = [nc.dram_tensor(f"w{g}_in", [D, D], F32, kind="ExternalInput").ap()
               for g in range(8)]
    bias_in = nc.dram_tensor("bias_in", [128, KB, 5], F32, kind="ExternalInput").ap()
    eye_f_in = nc.dram_tensor("eye_f_in", [128, 128], F32, kind="ExternalInput").ap()
    eye_b_in = nc.dram_tensor("eye_b_in", [128, 128], BF16, kind="ExternalInput").ap()
    sel_in = nc.dram_tensor("sel_in", [N, N * 128], BF16, kind="ExternalInput").ap()
    out_t = nc.dram_tensor("out", [B, TK, D], F32, kind="ExternalOutput").ap()

    ag1i = [nc.dram_tensor(f"ag1i_{b}", [96, TK], BF16).ap() for b in range(B)]
    ag1o = [nc.dram_tensor(f"ag1o_{b}", [96 * NC, TK], BF16).ap() for b in range(B)]
    ag2i = [nc.dram_tensor(f"ag2i_{b}", [D, TK], BF16).ap() for b in range(B)]
    ag2o = [nc.dram_tensor(f"ag2o_{b}", [D * NC, TK], BF16).ap() for b in range(B)]
    ag3i = [nc.dram_tensor(f"ag3i_{b}", [DK, L], BF16).ap() for b in range(B)]
    ag3o = [nc.dram_tensor(f"ag3o_{b}", [DK * NC, L], BF16).ap() for b in range(B)]
    xnts = nc.dram_tensor("xnts", [B, D, TK], BF16).ap()
    dump_dt = nc.dram_tensor("dump_dt", [B, 128, L], BF16, kind="ExternalOutput").ap() if DUMP else None
    dump_dtx = nc.dram_tensor("dump_dtx", [B, 128, L], BF16, kind="ExternalOutput").ap() if DUMP else None
    dump_br = nc.dram_tensor("dump_br", [128, CH], BF16, kind="ExternalOutput").ap() if DUMP else None
    dump_a = nc.dram_tensor("dump_a", [128, CH], F32, kind="ExternalOutput").ap() if DUMP else None
    dump_u = nc.dram_tensor("dump_u", [128, CH], BF16, kind="ExternalOutput").ap() if DUMP else None
    dump_h = nc.dram_tensor("dump_h", [128, CH], BF16, kind="ExternalOutput").ap() if DUMP else None
    dump_carry = nc.dram_tensor("dump_carry", [128, 16], F32, kind="ExternalOutput").ap() if DUMP else None
    dump_carry2 = nc.dram_tensor("dump_carry2", [128, 16], F32, kind="ExternalOutput").ap() if DUMP else None
    dump_cr = nc.dram_tensor("dump_cr", [128, CH], BF16, kind="ExternalOutput").ap() if DUMP else None
    dump_cc = nc.dram_tensor("dump_cc", [16, CH], BF16, kind="ExternalOutput").ap() if DUMP else None
    dump_xs = nc.dram_tensor("dump_xs", [128, CH], BF16, kind="ExternalOutput").ap() if DUMP else None
    if DUMP >= 5:
        dump_brall = nc.dram_tensor("dump_brall", [8, 16, 128, CH], BF16, kind="ExternalOutput").ap()
        dump_aall = nc.dram_tensor("dump_aall", [8, 16, 128, CH], F32, kind="ExternalOutput").ap()
        dump_uall = nc.dram_tensor("dump_uall", [8, 16, 128, CH], BF16, kind="ExternalOutput").ap()
        dump_hall = nc.dram_tensor("dump_hall", [8, 16, 128, CH], BF16, kind="ExternalOutput").ap()
        dump_crall = nc.dram_tensor("dump_crall", [8, 16, 128, CH], BF16, kind="ExternalOutput").ap()
        dump_pall = nc.dram_tensor("dump_pall", [8, 16, 128, CH], BF16, kind="ExternalOutput").ap()
        dump_yball = nc.dram_tensor("dump_yball", [8, 128, CH], BF16, kind="ExternalOutput").ap()
    wts = [nc.dram_tensor(f"wts_{g}", [D, D], BF16).ap() for g in range(8)]
    fence_d1 = nc.dram_tensor("fence_d1", [64, CH], BF16).ap()
    fence_d2 = nc.dram_tensor("fence_d2", [N, CH], BF16).ap()
    fence_d3 = nc.dram_tensor("fence_d3", [128, CH], BF16).ap()
    fence_d4 = nc.dram_tensor("fence_d4", [128, CH], BF16).ap()

    ctx = ExitStack()
    sb = lambda name, shape, dt: ctx.enter_context(nc.sbuf_tensor(name, shape, dt))

    wxT = sb("wxT", [128, KB, 96], BF16)
    wdtT = sb("wdtT", [64, 128], BF16)
    bdt_t = sb("bdt_t", [DK, 1], F32)
    alog_t = sb("alog_t", [DK, N], F32)
    a_mat = sb("a_mat", [DK, N], F32)
    dvec_t = sb("dvec_t", [DK, 1], F32)
    eye_f = sb("eye_f", [128, 128], F32)
    eye_b = sb("eye_b", [128, 128], BF16)
    sel_t = sb("sel_t", [N, N * 128], BF16)
    bias_t = sb("bias_t", [128, KB, 5], F32)
    ddiag = sb("ddiag", [128, 128], F32)
    dhi = sb("dhi", [128, 128], BF16)
    dlo = sb("dlo", [128, 128], BF16)

    wg_w = sb("wg_w", [128, KB, D], BF16)
    wg_u = sb("wg_u", [128, KB, D], BF16)

    xch = [sb(f"xch{i}", [128, D], F32) for i in range(2)]
    sqch = sb("sqch", [128, D], F32)
    xnf = sb("xnf", [128, D], F32)
    xnbf = sb("xnbf", [128, D], BF16)
    st_bn = sb("st_bn", [128, 12], F32)
    st_mv = sb("st_mv", [128, 2], F32)
    st_sum = sb("st_sum", [128, 1], F32)
    st_sq = sb("st_sq", [128, 1], F32)
    st_mean = sb("st_mean", [128, 1], F32)
    st_ex2 = sb("st_ex2", [128, 1], F32)
    st_m2 = sb("st_m2", [128, 1], F32)
    st_var = sb("st_var", [128, 1], F32)
    st_std = sb("st_std", [128, 1], F32)
    st_rstd = sb("st_rstd", [128, 1], F32)
    eps_t = sb("eps_t", [128, 1], F32)
    xnT = sb("xnT", [128, KB, TK], BF16)
    xdbf = sb("xdbf", [96, TK], BF16)

    c_a = sb("c_a", [128, KB, TK], BF16)
    arena = sb("arena", [128, 8192], F32)
    c_b = arena[:, 0:4096].bitcast(BF16).rearrange("p (k t) -> p k t", k=KB)
    gi_full = arena[:, 4096:8192].bitcast(BF16).rearrange("p (k t) -> p k t", k=KB)
    t1 = arena[:].rearrange("p (m t) -> p m t", m=16)
    gf_t = sb("gf_t", [128, 512], BF16)
    gg_t = sb("gg_t", [128, 512], BF16)
    cprod = sb("cprod", [128, 512], BF16)

    dtlo_r = [sb(f"dtlo{i}", [64, CH], BF16) for i in range(2)]
    ccomp_r = [sb(f"ccomp{i}", [N, CH], BF16) for i in range(2)]
    xnsl_r = [sb(f"xnsl{i}", [128, CH], BF16) for i in range(2)]
    dtbf_r = [sb(f"dtbf{i}", [128, CH], BF16) for i in range(2)]
    dtx_r = [sb(f"dtx{i}", [128, CH], BF16) for i in range(2)]
    et_t = sb("et_t", [128, 512], F32)
    dhi_f = et_t[:, 0:128]
    dlo_f = et_t[:, 128:256]
    a_ring = [sb(f"a_ring{i}", [128, CH], F32) for i in range(2)]
    brep = [sb(f"brep{i}", [128, CH], BF16) for i in range(3)]
    u_t = sb("u_t", [128, CH], BF16)
    h_t = sb("h_t", [128, CH], BF16)
    prod = [sb(f"prod{i}", [128, CH], BF16) for i in range(3)]
    crep = [sb(f"crep{i}", [128, CH], BF16) for i in range(2)]
    carry = sb("carry", [128, N], F32)
    ybf_r = [sb(f"ybf{i}", [128, CH], BF16) for i in range(2)]

    ybT = sb("ybT", [128, KB, TK], BF16)
    hch = [sb(f"hch{i}", [128, 512], F32) for i in range(2)]
    xch2 = xch
    outch = [xnf, sqch]

    pgate = ctx.enter_context(nc.psum_tensor("pgate", [128, 1024], F32))
    pdt = ctx.enter_context(nc.psum_tensor("pdt", [128, 1024], F32))
    pc = ctx.enter_context(nc.psum_tensor("pc", [128, 1024], F32))
    py = ctx.enter_context(nc.psum_tensor("py", [128, 1024], F32))

    Sm = lambda nm: Sem(nc, nm)
    x_ld = Sm("x_ld"); xt_ld = Sm("xt_ld"); misc_ld = Sm("misc_ld")
    w_ld = Sm("w_ld"); wt_tr = Sm("wt_tr"); wt_sp = Sm("wt_sp")
    v_xn = Sm("v_xn"); s_stat = Sm("s_stat"); v_stat = Sm("v_stat"); s_xnbf = Sm("s_xnbf")
    ag1w = Sm("ag1w"); ag2w = Sm("ag2w"); ag3w = Sm("ag3w"); xnts_w = Sm("xnts_w")
    cc1 = Sm("cc1"); cc2 = Sm("cc2"); cc3 = Sm("cc3")
    t_xd = Sm("t_xd"); s_xd = Sm("s_xd")
    g_ld = Sm("g_ld"); sc_ld = Sm("sc_ld")
    t_dt = Sm("t_dt"); s_dtexp = Sm("s_dtexp"); s_dt = Sm("s_dt"); v_dtx = Sm("v_dtx")
    g_brep = Sm("g_brep"); s_a = Sm("s_a"); v_u = Sm("v_u"); v_h = Sm("v_h")
    t_cr = Sm("t_cr"); s_cre = Sm("s_cre"); v_p = Sm("v_p"); t_p = Sm("t_p")
    t_yf = Sm("t_yf"); v_ye = Sm("v_ye"); t_D = Sm("t_D")
    t_gate = Sm("t_gate"); s_gate = Sm("s_gate"); v_c = Sm("v_c")
    wg_ld = Sm("wg_ld")
    t_wo = Sm("t_wo"); s_h = Sm("s_h"); v_t1 = Sm("v_t1"); g_yb = Sm("g_yb")
    t_tr = Sm("t_tr"); v_out = Sm("v_out"); st_out = Sm("st_out")
    s_prep = Sm("s_prep"); v_prep = Sm("v_prep"); v_drain = Sm("v_drain"); v_cz = Sm("v_cz"); d_dt = Sm("d_dt")
    v_ci = Sm("v_ci")
    t_xnt = Sm("t_xnt"); v_xnt = Sm("v_xnt")

    ops = []
    def op(eng, fn):
        ops.append((eng, fn))

    _dyn_cache = {}
    def dynslice(eng, size):
        key = size
        if key not in _dyn_cache:
            _dyn_cache[key] = bass.ts(eng.partition_id(), size)
        return _dyn_cache[key]

    RG = [list(range(NC))]
    gate_act_vals = []

    def ld_small(sc):
        for src, dst in ((bdt_in, bdt_t), (alog_in, alog_t), (dvec_in, dvec_t),
                         (eye_f_in, eye_f), (eye_b_in, eye_b), (sel_in, sel_t),
                         (bias_in, bias_t)):
            d = sc.dma_start(out=dst[:], in_=src)
            misc_ld.tag(d, 16)
    for _ in range(7):
        misc_ld.reserve(16)
    MV = misc_ld.count
    op("scalar", ld_small)

    def ld_cast0(g):
        w_ld.tag(g.dma_start(out=wxT[:], in_=wx_in.rearrange("(k p) c -> p k c", p=128)), 16)
        w_ld.tag(g.dma_start(out=wdtT[:], in_=wdt_in), 16)
    w_ld.reserve(32)
    W2 = w_ld.count
    op("gpsimd", ld_cast0)

    def prep_v(v):
        v_prep.tag(v.memset(eps_t[:], 1e-5))
    VP_MEMSET = v_prep.reserve()
    op("vector", prep_v)

    def prep_s(s):
        s.wait_ge(misc_ld.h, MV)
        s_prep.tag(s.activation(a_mat[:], alog_t[:], AF.Exp))
    SP0 = s_prep.reserve()
    op("scalar", prep_s)

    def prep_v2(v):
        v.wait_ge(s_prep.h, SP0)
        v.tensor_scalar(a_mat[:], a_mat[:], -1.0, None, ALU.mult)
        v_prep.tag(v.tensor_scalar(ddiag[:], eye_f[:], dvec_t[:], None, ALU.mult))
    VP0 = v_prep.reserve()
    op("vector", prep_v2)

    def prep_s2(s):
        s.wait_ge(v_prep.h, VP0)
        s.copy(dhi[:], ddiag[:])
        s_prep.tag(s.copy(dhi_f[:], dhi[:]))
    SP1 = s_prep.reserve()
    op("scalar", prep_s2)

    def prep_v3(v):
        v.wait_ge(s_prep.h, SP1)
        v_prep.tag(v.tensor_sub(dlo_f[:], ddiag[:], dhi_f[:]))
    VP1 = v_prep.reserve()
    op("vector", prep_v3)

    def prep_s3(s):
        s.wait_ge(v_prep.h, VP1)
        s_prep.tag(s.copy(dlo[:], dlo_f[:]))
    SPF = s_prep.reserve()
    op("scalar", prep_s3)

    WT_BASE = 0

    def weight_triplet(g):
        sp_guard = wt_sp.count
        def ld_w(gp, g=g, sp_guard=sp_guard, nb=g // 2 + 1):
            gp.wait_ge(cc2.h, nb)
            gp.wait_ge(cc1.h, nb)
            if sp_guard > 0:
                gp.wait_ge(wt_sp.h, sp_guard)
            w_ld.tag(gp.dma_start(out=wg_u[:], in_=wbig_in[g].rearrange("(k p) c -> p k c", p=128)), 16)
        wld_v = w_ld.reserve(16)
        op("gpsimd", ld_w)
        def sp_w(sc, g=g, wld_v=wld_v):
            sc.wait_ge(w_ld.h, wld_v)
            wt_sp.tag(sc.dma_start(out=wts[g].rearrange("(i p) c -> p i c", p=128), in_=wg_u[:]), 16)
        wt_sp.reserve(16)
        op("scalar", sp_w)

    for b in range(B):
        for q in range(KB):
            jq = b * KB + q
            slot = q % 2
            xc = xch[slot]
            vg = v_xn.count - 1 if jq >= 2 else -1
            xval = x_ld.reserve(16)
            def ldx(sy, b=b, q=q, xc=xc, vg=vg):
                if vg >= 0:
                    sy.wait_ge(v_xn.h, vg)
                x_ld.tag(sy.dma_start(out=xc[:], in_=x_in[b, 128 * q:128 * (q + 1), :]), 16)
            op("sync", ldx)
            sstd_guard = s_stat.count
            d1 = v_drain.reserve(); d2 = v_drain.reserve(); d3 = v_drain.reserve()
            vstat_v = v_stat.reserve()
            def stats_v(v, xc=xc, xval=xval, d1=d1, d2=d2, d3=d3,
                        sstd_guard=sstd_guard, jq_=jq):
                v.wait_ge(x_ld.h, xval)
                if sstd_guard > 0:
                    v.wait_ge(s_stat.h, sstd_guard)
                v_drain.tag(v.bn_stats(st_bn[:, 0:6], xc[:, 0:512]))
                v.wait_ge(v_drain.h, d1)
                v_drain.tag(v.bn_stats(st_bn[:, 6:12], xc[:, 512:1024]))
                v.wait_ge(v_drain.h, d2)
                v_drain.tag(v.bn_aggr(st_mv[:], st_bn[:]))
                v.wait_ge(v_drain.h, d3)
                v_stat.tag(v.tensor_copy(st_mean[:], st_mv[:, 0:1]))
            op("vector", stats_v)
            vxn_guard = v_xn.count
            sstd_v = s_stat.reserve()
            def std_s(s, vstat_v=vstat_v, vxn_guard=vxn_guard):
                s.wait_ge(v_stat.h, vstat_v)
                if vxn_guard > 0:
                    s.wait_ge(v_xn.h, vxn_guard)
                s.wait_ge(v_prep.h, 1)
                s.activation(st_std[:], st_mv[:, 1:2], AF.Ln, bias=eps_t[:])
                s_stat.tag(s.activation(st_rstd[:], st_std[:], AF.Exp, scale=-0.5))
            op("scalar", std_s)
            sbf_guard = s_xnbf.count
            vxn_v = v_xn.reserve()
            def norm_v(v, xc=xc, sstd_v=sstd_v, sbf_guard=sbf_guard):
                v.wait_ge(s_stat.h, sstd_v)
                if sbf_guard > 0:
                    v.wait_ge(s_xnbf.h, sbf_guard)
                v_xn.tag(v.tensor_scalar(xnf[:], xc[:], st_mean[:], st_rstd[:],
                                         ALU.subtract, ALU.mult))
            op("vector", norm_v)
            xt_guard = t_xnt.count
            sbf_v = s_xnbf.reserve()
            def tobf_s(s, vxn_v=vxn_v, xt_guard=xt_guard):
                s.wait_ge(v_xn.h, vxn_v)
                if xt_guard > 0:
                    s.wait_ge(t_xnt.h, xt_guard)
                s_xnbf.tag(s.copy(xnbf[:], xnf[:]))
            op("scalar", tobf_s)
            txd_guard = t_xd.count
            vxnt_guard = v_xnt.count
            txnt_v = t_xnt.reserve()
            def tr_xn(t, q=q, sbf_v=sbf_v, txd_guard=txd_guard, vxnt_guard=vxnt_guard):
                t.wait_ge(s_xnbf.h, sbf_v)
                if vxnt_guard > 0:
                    t.wait_ge(v_xnt.h, vxnt_guard)
                pcb = pc[:].bitcast(BF16)
                for k in range(KB):
                    mm = t.transpose(pcb[:, 128 * k:128 * (k + 1)],
                                     xnbf[:, 128 * k:128 * (k + 1)], eye_b[:])
                t_xnt.tag(mm)
            op("tensor", tr_xn)
            vxnt_v = v_xnt.reserve()
            def cp_xn(v, b=b, q=q, txnt_v=txnt_v, txd_guard=txd_guard):
                v.wait_ge(t_xnt.h, txnt_v)
                if q == 0 and txd_guard > 0:
                    v.wait_ge(t_xd.h, txd_guard)
                if q == 0 and b > 0:
                    v.wait_ge(ag2w.h, 16 * b)
                    v.wait_ge(xnts_w.h, 16 * b)
                v_xnt.tag(v.tensor_copy(xnT[:, :, 128 * q:128 * (q + 1)],
                                        pc[:].bitcast(BF16)[:, 0:1024].rearrange("p (k c) -> p k c", k=KB)))
            op("vector", cp_xn)
        xt_v = v_xnt.count
        ag2_v = ag2w.reserve(16)
        xnts_w.reserve(16)
        def wr_xn(sy, b=b, xt_v=xt_v):
            sy.wait_ge(v_xnt.h, xt_v)
            ag2w.tag(sy.dma_start(out=ag2i[b].rearrange("(k p) t -> p k t", p=128), in_=xnT[:]), 16)
            xnts_w.tag(sy.dma_start(out=xnts[b].rearrange("(k p) t -> p k t", p=128), in_=xnT[:]), 16)
        op("scalar", wr_xn)
        cc2.reserve(1)
        def coll2(g, b=b, ag2_v=ag2_v, wld_all=w_ld.count, sp_all=wt_sp.count):
            g.wait_ge(ag2w.h, ag2_v)
            g.wait_ge(w_ld.h, wld_all)
            if sp_all > 0:
                g.wait_ge(wt_sp.h, sp_all)
            if b > 0:
                g.wait_ge(cc1.h, b)
            cc2.tag(g.collective_compute("AllGather", ALU.bypass, replica_groups=RG,
                                         ins=[ag2i[b]], outs=[ag2o[b]]))
        op("gpsimd", coll2)
        for h2 in range(2):
            sxd_g = s_xd.count
            txd_v = t_xd.reserve()
            first_t = (b == 0 and h2 == 0)
            def mm_xd(t, h2=h2, xt_v=xt_v, sxd_g=sxd_g, first_t=first_t):
                if first_t:
                    t.wait_ge(misc_ld.h, MV)
                    t.wait_ge(s_prep.h, SPF)
                t.wait_ge(v_xnt.h, xt_v)
                t.wait_ge(w_ld.h, W2)
                if sxd_g > 0:
                    t.wait_ge(s_xd.h, sxd_g)
                sl = slice(512 * h2, 512 * (h2 + 1))
                for k in range(KB):
                    mm = t.matmul(pdt[0:96, sl], wxT[:, k, :], xnT[:, k, sl],
                                  start=(k == 0), stop=(k == KB - 1))
                t_xd.tag(mm)
            op("tensor", mm_xd)
            s_xd.reserve()
            def ev_xd(s, b=b, h2=h2, txd_v=txd_v):
                s.wait_ge(t_xd.h, txd_v)
                if h2 == 0 and b > 0:
                    s.wait_ge(ag1w.h, 16 * b)
                sl = slice(512 * h2, 512 * (h2 + 1))
                s_xd.tag(s.copy(xdbf[:, sl], pdt[0:96, sl]))
            op("scalar", ev_xd)
        sxd_v = s_xd.count
        ag1_v = ag1w.reserve(16)
        def wr_xd(g, b=b, sxd_v=sxd_v):
            g.wait_ge(s_xd.h, sxd_v)
            ag1w.tag(g.dma_start(out=ag1i[b], in_=xdbf[:]), 16)
        op("scalar", wr_xd)
        cc1.reserve(1)
        def coll1(g, b=b, ag1_v=ag1_v, wld_all=w_ld.count, sp_all=wt_sp.count):
            g.wait_ge(ag1w.h, ag1_v)
            g.wait_ge(w_ld.h, wld_all)
            if sp_all > 0:
                g.wait_ge(wt_sp.h, sp_all)
            g.wait_ge(cc2.h, b + 1)
            cc1.tag(g.collective_compute("AllGather", ALU.bypass, replica_groups=RG,
                                         ins=[ag1i[b]], outs=[ag1o[b]]))
        op("gpsimd", coll1)
        weight_triplet(2 * b)
        weight_triplet(2 * b + 1)


    X_DBL_DONE = t_xd.count
    P1_AG2 = ag2w.count
    P1_XNTS = xnts_w.count
    P1_XNT = v_xnt.count
    ALL_SPILLS = wt_sp.count

    def make_ffn_step(b):
        c_old = c_a if b % 2 == 0 else c_b
        c_new = c_b if b % 2 == 0 else c_a
        units = []
        tgate_guard = t_gate.count
        xn_v = sc_ld.reserve(16)
        def ld_xn(sc, b=b, tgate_guard=tgate_guard):
            if b == 0:
                sc.wait_ge(t_xd.h, X_DBL_DONE)
                sc.wait_ge(ag2w.h, P1_AG2)
                sc.wait_ge(xnts_w.h, P1_XNTS)
            else:
                sc.wait_ge(t_gate.h, tgate_guard)
            sc.wait_ge(xnts_w.h, 16 * (b + 1))
            sc_ld.tag(sc.dma_start(out=xnT[:], in_=xnts[b].rearrange("(k p) t -> p k t", p=128)), 16)
        units.append(("scalar", ld_xn))
        cinit_v = -1
        if b == 0:
            cinit_v = v_c.reserve()
            vci_v = v_ci.reserve()
            def init_c(v, vci_v=vci_v):
                v.wait_ge(misc_ld.h, MV)
                v_ci.tag(v.memset(c_a[:], 0.0))
                v.wait_ge(v_ci.h, vci_v)
                for k in range(KB - 1):
                    v.tensor_scalar(c_a[:, k, :], c_a[:, k, :], bias_t[:, k, 4:5],
                                    None, ALU.add)
                k = KB - 1
                v_c.tag(v.tensor_scalar(c_a[:, k, :], c_a[:, k, :], bias_t[:, k, 4:5],
                                        None, ALU.add))
            units.append(("vector", init_c))
        for gi in range(3):
            func_is_sig = gi < 2
            bj = gi
            wld_guard = t_gate.count
            wgl_v = wg_ld.reserve(32)
            def ld_wg(sc, gi=gi, wld_guard=wld_guard):
                if wld_guard > 0:
                    sc.wait_ge(t_gate.h, wld_guard)
                sc.wait_ge(wt_sp.h, ALL_SPILLS)
                wg_ld.tag(sc.dma_start(out=wg_w[:], in_=wts[2 * gi].rearrange("(k p) c -> p k c", p=128)), 16)
                wg_ld.tag(sc.dma_start(out=wg_u[:], in_=wts[2 * gi + 1].rearrange("(k p) c -> p k c", p=128)), 16)
            units.append(("scalar", ld_wg))
            for m in range(KB):
                for h2 in range(2):
                    uglobal = len(gate_act_vals)
                    half_guard = gate_act_vals[uglobal - 2] if uglobal >= 2 else -1
                    vc_guard = v_c.count
                    tg_v = t_gate.reserve()
                    def mm_g(t, gi=gi, m=m, h2=h2, b=b, wgl_v=wgl_v, xn_v=xn_v,
                             half_guard=half_guard, c_old=c_old, vc_guard=vc_guard,
                             cinit_v=cinit_v):
                        t.wait_ge(wg_ld.h, wgl_v)
                        t.wait_ge(sc_ld.h, xn_v)
                        if half_guard >= 0:
                            t.wait_ge(s_gate.h, half_guard)
                        if b == 0:
                            t.wait_ge(v_c.h, cinit_v)
                        elif vc_guard > 0:
                            t.wait_ge(v_c.h, vc_guard)
                        sl = slice(512 * h2, 512 * (h2 + 1))
                        for k in range(KB):
                            t.matmul(pgate[:, sl], wg_w[:, k, 128 * m:128 * (m + 1)],
                                     xnT[:, k, sl], start=(k == 0), stop=False)
                        for k in range(KB):
                            mm = t.matmul(pgate[:, sl], wg_u[:, k, 128 * m:128 * (m + 1)],
                                          c_old[:, k, sl], start=False, stop=(k == KB - 1))
                        t_gate.tag(mm)
                    units.append(("tensor", mm_g))
                    vc_g2 = v_c.count
                    sg_v = s_gate.reserve()
                    gate_act_vals.append(sg_v)
                    def act_g(s, gi=gi, m=m, h2=h2, bj=bj, tg_v=tg_v, vc_g2=vc_g2,
                              func_is_sig=func_is_sig):
                        s.wait_ge(t_gate.h, tg_v)
                        if gi in (0, 2) and vc_g2 > 0:
                            s.wait_ge(v_c.h, vc_g2)
                        sl = slice(512 * h2, 512 * (h2 + 1))
                        if gi == 0:
                            dst = gf_t[:]
                        elif gi == 1:
                            dst = gi_full[:, m, sl]
                        else:
                            dst = gg_t[:]
                        s_gate.tag(s.activation(dst, pgate[:, sl],
                                                AF.Sigmoid if func_is_sig else AF.Tanh,
                                                bias=bias_t[:, m, bj:bj + 1]))
                    units.append(("scalar", act_g))
                    if gi == 0:
                        v_c.reserve()
                        def upd_f(v, m=m, h2=h2, sg_v=sg_v, c_old=c_old, c_new=c_new):
                            v.wait_ge(s_gate.h, sg_v)
                            sl = slice(512 * h2, 512 * (h2 + 1))
                            v_c.tag(v.tensor_mul(c_new[:, m, sl], gf_t[:], c_old[:, m, sl]))
                        units.append(("vector", upd_f))
                    elif gi == 2:
                        v_c.reserve()
                        def upd_g(v, m=m, h2=h2, sg_v=sg_v, c_new=c_new):
                            v.wait_ge(s_gate.h, sg_v)
                            sl = slice(512 * h2, 512 * (h2 + 1))
                            v.tensor_mul(cprod[:], gi_full[:, m, sl], gg_t[:])
                            v_c.tag(v.tensor_add(c_new[:, m, sl], c_new[:, m, sl], cprod[:]))
                        units.append(("vector", upd_g))
        return units

    for b in (range(B) if STAGE >= 2 else []):
        cc1_v = b + 1
        cc2_v = b + 1
        ffn_units = make_ffn_step(b)
        fu_iter = iter(ffn_units)
        fu_per_chunk = ((len(ffn_units) + NCH - 1) // NCH) if SUB >= 3 else 0

        vcz_v = v_cz.reserve()
        def zero_carry(v, b=b, vcz_v=vcz_v):
            v_cz.tag(v.memset(carry[:], 0.0))
            v.wait_ge(v_cz.h, vcz_v)
        op("vector", zero_carry)
        for c8 in range(NCH):
            it_base = b * NCH + c8
            rslot = c8 % 2
            tdt_guard = t_dt.count - 2 if it_base >= 2 else -1
            tcr_guard = t_cr.count - 16 if it_base >= 2 else -1
            vdtx_guard = v_dtx.count - 1 if it_base >= 2 else -1
            tD_guard = t_D.count - 1 if it_base >= 2 else -1
            gld_v = g_ld.reserve(96)
            ddt_g = d_dt.count
            def ld_chunk(g, b=b, c8=c8, rslot=rslot, cc1_v=cc1_v,
                         tdt_guard=tdt_guard, tcr_guard=tcr_guard, ddt_g=ddt_g):
                g.wait_ge(cc1.h, cc1_v)
                if tdt_guard >= 0:
                    g.wait_ge(t_dt.h, tdt_guard)
                if tcr_guard >= 0:
                    g.wait_ge(t_cr.h, tcr_guard)
                if DUMP and ddt_g > 0:
                    g.wait_ge(d_dt.h, ddt_g)
                g_ld.tag(g.dma_start(out=dtlo_r[rslot][:],
                                     in_=ag1o[b][96 * c8:96 * c8 + 64, :]), 16)
                g_ld.tag(g.dma_start(out=ccomp_r[rslot][:],
                                     in_=ag1o[b][96 * c8 + 80:96 * (c8 + 1), :]), 16)
            op("sync", ld_chunk)
            def ld_xnsl(sy, b=b, c8=c8, rslot=rslot, cc2_v=cc2_v,
                        vdtx_guard=vdtx_guard, tD_guard=tD_guard):
                sy.wait_ge(cc2.h, cc2_v)
                if vdtx_guard >= 0:
                    sy.wait_ge(v_dtx.h, vdtx_guard)
                if tD_guard >= 0:
                    sy.wait_ge(t_D.h, tD_guard)
                src2 = ag2o[b].rearrange("(r z) t -> z r t", z=D)[:, c8, :]
                g_ld.tag(sy.dma_start(out=xnsl_r[rslot][:],
                                      in_=src2[dynslice(sy, 128), :]), 16)
                g_ld.tag(sy.dma_start(out=fence_d1, in_=dtlo_r[rslot][:]), 16)
                g_ld.tag(sy.dma_start(out=fence_d2, in_=ccomp_r[rslot][:]), 16)
                g_ld.tag(sy.dma_start(out=fence_d3, in_=xnsl_r[rslot][:]), 16)
            op("sync", ld_xnsl)
            for half in range(2):
                uu = it_base * 2 + half
                sdte_g = s_dtexp.count - 1 if uu >= 2 else -1
                tdt_v = t_dt.reserve()
                def mm_dt(t, rslot=rslot, half=half, gld_v=gld_v, sdte_g=sdte_g):
                    t.wait_ge(g_ld.h, gld_v)
                    t.wait_ge(s_xd.h, X_DBL_DONE)
                    if sdte_g >= 0:
                        t.wait_ge(s_dtexp.h, sdte_g)
                    sl = slice(512 * half, 512 * (half + 1))
                    t_dt.tag(t.matmul(pdt[:, sl], wdtT[0:64, :], dtlo_r[rslot][:, sl],
                                      start=True, stop=True))
                op("tensor", mm_dt)
                vdtx_g2 = v_dtx.count - 1 if it_base >= 2 else -1
                s_dtexp.reserve()
                sdt_half = s_dt.reserve()
                def sp_dt(s, rslot=rslot, half=half, tdt_v=tdt_v, vdtx_g2=vdtx_g2):
                    s.wait_ge(t_dt.h, tdt_v)
                    s.wait_ge(misc_ld.h, MV)
                    if half == 0 and vdtx_g2 >= 0:
                        s.wait_ge(v_dtx.h, vdtx_g2)
                    sl = slice(512 * half, 512 * (half + 1))
                    s_dtexp.tag(s.activation(et_t[:], pdt[:, sl], AF.Exp, bias=bdt_t[:]))
                    s_dt.tag(s.activation(dtbf_r[rslot][:, sl], et_t[:], AF.Ln, bias=1.0))
                op("scalar", sp_dt)
            sdt_v = s_dt.count
            if DUMP:
                d_dt.reserve(16)
                def dmp_dt(sy, b=b, c8=c8, rslot=rslot, sdt_v=sdt_v):
                    sy.wait_ge(s_dt.h, sdt_v)
                    d_dt.tag(sy.dma_start(out=dump_dt[b][:, CH * c8:CH * (c8 + 1)],
                                          in_=dtbf_r[rslot][:]), 16)
                op("sync", dmp_dt)
            vdtx_v = v_dtx.reserve()
            def mk_dtx(v, rslot=rslot, sdt_v=sdt_v, gld_v=gld_v):
                v.wait_ge(s_dt.h, sdt_v)
                v.wait_ge(g_ld.h, gld_v)
                v_dtx.tag(v.tensor_mul(dtx_r[rslot][:], dtbf_r[rslot][:], xnsl_r[rslot][:]))
            op("vector", mk_dtx)
            if DUMP:
                d_dt.reserve(16)
                def dmp_dtx(sy, b=b, c8=c8, rslot=rslot, vdtx_v=vdtx_v):
                    sy.wait_ge(v_dtx.h, vdtx_v)
                    d_dt.tag(sy.dma_start(out=dump_dtx[b][:, CH * c8:CH * (c8 + 1)],
                                          in_=dtx_r[rslot][:]), 16)
                op("sync", dmp_dtx)
            vye_g = v_ye.count
            t_D.reserve()
            def mm_D(t, rslot=rslot, vye_g=vye_g, gld_v=gld_v):
                if vye_g > 0:
                    t.wait_ge(v_ye.h, vye_g)
                t.wait_ge(g_ld.h, gld_v)
                for half in range(2):
                    sl = slice(512 * half, 512 * (half + 1))
                    t.matmul(py[:, sl], dhi[:], xnsl_r[rslot][:, sl], start=True, stop=False)
                    mm = t.matmul(py[:, sl], dlo[:], xnsl_r[rslot][:, sl], start=False, stop=False)
                t_D.tag(mm)
            op("tensor", mm_D)
            dgate = 0
            for n in (range(N) if SUB >= 1 else []):
                it = it_base * N + n
                bslot = it % 3
                vu_g = v_u.count - 2 if it >= 3 else -1
                gb_v = g_brep.reserve(16)
                def bro(g, b=b, c8=c8, n=n, bslot=bslot, vu_g=vu_g, cc1_v=cc1_v, dg=dgate):
                    g.wait_ge(cc1.h, cc1_v)
                    if dg > 0:
                        g.wait_ge(d_dt.h, dg)
                    if vu_g >= 0:
                        g.wait_ge(v_u.h, vu_g)
                    row = 96 * c8 + 64 + n
                    g_brep.tag(g.dma_start(out=brep[bslot][:],
                                           in_=ag1o[b][row:row + 1, :].partition_broadcast(128)), 16)
                op("sync", bro)
                if n == N - 1:
                    g_brep.reserve(16)
                    def bro_fence(sy, bslot=bslot):
                        g_brep.tag(sy.dma_start(out=fence_d4, in_=brep[bslot][:]), 16)
                    op("sync", bro_fence)
                aslot = it % 2
                vh_g = v_h.count - 1 if it >= 2 else -1
                sa_v = s_a.reserve()
                def mk_a(s, n=n, rslot=rslot, aslot=aslot, vh_g=vh_g, sdt_v=sdt_v, dg=dgate):
                    s.wait_ge(s_dt.h, sdt_v)
                    if dg > 0:
                        s.wait_ge(d_dt.h, dg)
                    if vh_g >= 0:
                        s.wait_ge(v_h.h, vh_g)
                    s_a.tag(s.activation(a_ring[aslot][:], dtbf_r[rslot][:], AF.Exp,
                                         scale=a_mat[:, n:n + 1]))
                op("scalar", mk_a)
                v_u.reserve()
                vh_v = v_h.reserve()
                def scan_v(v, n=n, rslot=rslot, bslot=bslot, aslot=aslot, gb_v=gb_v,
                           sa_v=sa_v, vdtx_v=vdtx_v, vh_v=vh_v, vcz_v=vcz_v, first=(c8 == 0),
                           dg=dgate):
                    if dg > 0:
                        v.wait_ge(d_dt.h, dg)
                    if first:
                        v.wait_ge(v_cz.h, vcz_v)
                    v.wait_ge(v_dtx.h, vdtx_v)
                    v.wait_ge(g_brep.h, gb_v + (0 if DUMP else 16))
                    v_u.tag(v.tensor_mul(u_t[:], dtx_r[rslot][:], brep[bslot][:]))
                    v.wait_ge(s_a.h, sa_v)
                    v_h.tag(v.tensor_tensor_scan(h_t[:], a_ring[aslot][:], u_t[:],
                                                 carry[:, n:n + 1], ALU.mult, ALU.add))
                    v.wait_ge(v_h.h, vh_v)
                    v.tensor_copy(carry[:, n:n + 1], h_t[:, CH - 1:CH])
                op("vector", scan_v)
                scre_g = s_cre.count if it >= 1 else -1
                tcr_v = t_cr.reserve()
                first_cr = (it == 0 and b == 0)
                def mk_cr(t, n=n, rslot=rslot, gld_v=gld_v, scre_g=scre_g, first_cr=first_cr):
                    t.wait_ge(g_ld.h, gld_v)
                    if first_cr:
                        t.wait_ge(v_xnt.h, P1_XNT)
                    if scre_g >= 0:
                        t.wait_ge(s_cre.h, scre_g)
                    for half in range(2):
                        sl = slice(512 * half, 512 * (half + 1))
                        mm = t.matmul(pc[:, sl], sel_t[:, 128 * n:128 * (n + 1)],
                                      ccomp_r[rslot][:, sl], start=True, stop=True)
                    t_cr.tag(mm)
                op("tensor", mk_cr)
                cslot = it % 2
                vp_g = v_p.count - 1 if it >= 2 else -1
                scre_v = s_cre.reserve()
                def ev_cr(s, cslot=cslot, tcr_v=tcr_v, vp_g=vp_g, dg=dgate):
                    s.wait_ge(t_cr.h, tcr_v)
                    if dg > 0:
                        s.wait_ge(d_dt.h, dg)
                    if vp_g >= 0:
                        s.wait_ge(v_p.h, vp_g)
                    s_cre.tag(s.copy(crep[cslot][:], pc[:]))
                op("scalar", ev_cr)
                pslot = it % 3
                tp_g = t_p.count - 2 if it >= 3 else -1
                vp_v = v_p.reserve()
                def mk_p(v, pslot=pslot, cslot=cslot, scre_v=scre_v, tp_g=tp_g, dg=dgate):
                    v.wait_ge(s_cre.h, scre_v)
                    if dg > 0:
                        v.wait_ge(d_dt.h, dg)
                    if tp_g >= 0:
                        v.wait_ge(t_p.h, tp_g)
                    v_p.tag(v.tensor_mul(prod[pslot][:], h_t[:], crep[cslot][:]))
                op("vector", mk_p)
                if DUMP and b == 0 and c8 == 0 and n == 0:
                    d_dt.reserve(16 * 8)
                    def dmp_sc(sy, bslot=bslot, aslot=aslot, cslot=cslot, rslot=rslot,
                               vh_v=vh_v, scre_v=scre_v, vp_v=vp_v):
                        sy.wait_ge(v_h.h, vh_v)
                        sy.wait_ge(s_cre.h, scre_v)
                        sy.wait_ge(v_p.h, vp_v)
                        d_dt.tag(sy.dma_start(out=dump_br, in_=brep[bslot][:]), 16)
                        d_dt.tag(sy.dma_start(out=dump_a, in_=a_ring[aslot][:]), 16)
                        d_dt.tag(sy.dma_start(out=dump_u, in_=u_t[:]), 16)
                        d_dt.tag(sy.dma_start(out=dump_h, in_=h_t[:]), 16)
                        d_dt.tag(sy.dma_start(out=dump_carry, in_=carry[:]), 16)
                        d_dt.tag(sy.dma_start(out=dump_cr, in_=crep[cslot][:]), 16)
                        d_dt.tag(sy.dma_start(out=dump_cc, in_=ccomp_r[rslot][:]), 16)
                        d_dt.tag(sy.dma_start(out=dump_xs, in_=xnsl_r[rslot][:]), 16)
                    op("sync", dmp_sc)
                    dgate = d_dt.count
                if DUMP >= 5 and b == 0:
                    d_dt.reserve(16 * 6)
                    def dmp_all(sy, c8=c8, n=n, bslot=bslot, aslot=aslot, cslot=cslot,
                                pslot=pslot, vh_v=vh_v, scre_v=scre_v, vp_v=vp_v):
                        sy.wait_ge(v_h.h, vh_v)
                        sy.wait_ge(s_cre.h, scre_v)
                        sy.wait_ge(v_p.h, vp_v)
                        d_dt.tag(sy.dma_start(out=dump_brall[c8, n], in_=brep[bslot][:]), 16)
                        d_dt.tag(sy.dma_start(out=dump_aall[c8, n], in_=a_ring[aslot][:]), 16)
                        d_dt.tag(sy.dma_start(out=dump_uall[c8, n], in_=u_t[:]), 16)
                        d_dt.tag(sy.dma_start(out=dump_hall[c8, n], in_=h_t[:]), 16)
                        d_dt.tag(sy.dma_start(out=dump_crall[c8, n], in_=crep[cslot][:]), 16)
                        d_dt.tag(sy.dma_start(out=dump_pall[c8, n], in_=prod[pslot][:]), 16)
                    op("sync", dmp_all)
                    dgate = d_dt.count
                if DUMP and b == 0 and c8 == 0 and n == N - 1:
                    d_dt.reserve(16)
                    def dmp_cy2(sy, vp_v=vp_v):
                        sy.wait_ge(v_p.h, vp_v)
                        d_dt.tag(sy.dma_start(out=dump_carry2, in_=carry[:]), 16)
                    op("sync", dmp_cy2)
                    dgate = d_dt.count
                if SUB < 2 or SUBY < 1:
                    continue
                t_p.reserve()
                def mk_y(t, n=n, pslot=pslot, vp_v=vp_v):
                    t.wait_ge(v_p.h, vp_v)
                    for half in range(2):
                        sl = slice(512 * half, 512 * (half + 1))
                        mm = t.matmul(py[:, sl], eye_b[:], prod[pslot][:, sl],
                                      start=False, stop=(n == N - 1))
                    t_p.tag(mm)
                op("tensor", mk_y)
            if SUB < 2 or SUBY < 2:
                continue
            tyf_v = t_p.count
            yslot = c8 % 2
            ag3_guard = ag3w.count - 16 if it_base >= 2 else -1
            vye_v = v_ye.reserve()
            def ev_y(v, yslot=yslot, tyf_v=tyf_v, ag3_guard=ag3_guard, dg=dgate):
                v.wait_ge(t_p.h, tyf_v)
                if dg > 0:
                    v.wait_ge(d_dt.h, dg)
                if ag3_guard >= 0:
                    v.wait_ge(ag3w.h, ag3_guard)
                v_ye.tag(v.tensor_copy(ybf_r[yslot][:], py[:]))
            op("vector", ev_y)
            if DUMP >= 5 and b == 0:
                d_dt.reserve(16)
                def dmp_ybf(sy, c8=c8, yslot=yslot, vye_v=vye_v):
                    sy.wait_ge(v_ye.h, vye_v)
                    d_dt.tag(sy.dma_start(out=dump_yball[c8], in_=ybf_r[yslot][:]), 16)
                op("sync", dmp_ybf)
                dgate = d_dt.count
            ag3w.reserve(16)
            def wr_y(sy, b=b, c8=c8, yslot=yslot, vye_v=vye_v):
                sy.wait_ge(v_ye.h, vye_v)
                ag3w.tag(sy.dma_start(out=ag3i[b][:, CH * c8:CH * (c8 + 1)],
                                      in_=ybf_r[yslot][:]), 16)
            op("scalar", wr_y)
            for _ in range(fu_per_chunk):
                try:
                    ops.append(next(fu_iter))
                except StopIteration:
                    break
        if SUB >= 3:
            for rest in fu_iter:
                ops.append(rest)
        if SUB < 2 or SUBY < 3:
            continue
        ag3_v = ag3w.count
        cc3.reserve(1)
        def coll3(g, b=b, ag3_v=ag3_v, wld_all=w_ld.count, sp_all=wt_sp.count):
            g.wait_ge(ag3w.h, ag3_v)
            if b == 0:
                g.wait_ge(cc2.h, B)
                g.wait_ge(cc1.h, B)
                g.wait_ge(w_ld.h, wld_all)
                g.wait_ge(wt_sp.h, sp_all)
            else:
                g.wait_ge(cc3.h, b)
            cc3.tag(g.collective_compute("AllGather", ALU.bypass, replica_groups=RG,
                                         ins=[ag3i[b]], outs=[ag3o[b]]))
        op("gpsimd", coll3)

    if STAGE < 3:
        def fin_stub(g):
            g.wait_ge(cc2.h, cc2.count)
            g.wait_ge(cc1.h, cc1.count)
            if STAGE >= 2:
                g.wait_ge(cc3.h, cc3.count)
        op("gpsimd", fin_stub)
    c_final = c_a if B % 2 == 0 else c_b
    if STAGE >= 3:
        TG_FINAL = t_gate.count
        wo_v = wg_ld.reserve(32)
        def ld_wo(sc):
            sc.wait_ge(t_gate.h, TG_FINAL)
            sc.wait_ge(wt_sp.h, ALL_SPILLS)
            wg_ld.tag(sc.dma_start(out=wg_w[:], in_=wts[6].rearrange("(k p) c -> p k c", p=128)), 16)
            wg_ld.tag(sc.dma_start(out=wg_u[:], in_=wts[7].rearrange("(k p) c -> p k c", p=128)), 16)
        op("scalar", ld_wo)
        VC_FINAL = v_c.count
        for b in range(B):
            two_guard = t_wo.count
            xn3_v = sc_ld.reserve(16)
            def ld_xn3(sc, b=b, two_guard=two_guard):
                if two_guard > 0:
                    sc.wait_ge(t_wo.h, two_guard)
                else:
                    sc.wait_ge(t_gate.h, TG_FINAL)
                sc_ld.tag(sc.dma_start(out=xnT[:], in_=xnts[b].rearrange("(k p) t -> p k t", p=128)), 16)
            op("scalar", ld_xn3)
            vt1_guard = v_t1.count
            gyb_v = g_yb.reserve(16)
            def ld_yb(sy, b=b, vt1_guard=vt1_guard):
                sy.wait_ge(cc3.h, b + 1)
                if vt1_guard > 0:
                    sy.wait_ge(v_t1.h, vt1_guard)
                g_yb.tag(sy.dma_start(out=ybT[:],
                                      in_=ag3o[b].rearrange("(k p) t -> p k t", p=128)[:, :, dynslice(sy, TK)]), 16)
            op("sync", ld_yb)
            for m in range(KB):
                for h2 in range(2):
                    j = m * 2 + h2
                    sh_guard = s_h.count - 1 if s_h.count >= 1 else -1
                    two_v = t_wo.reserve()
                    def mm_wo(t, m=m, h2=h2, sh_guard=sh_guard, c_final=c_final,
                              xn3_v=xn3_v):
                        t.wait_ge(wg_ld.h, wo_v)
                        t.wait_ge(sc_ld.h, xn3_v)
                        t.wait_ge(v_c.h, VC_FINAL)
                        if sh_guard >= 1:
                            t.wait_ge(s_h.h, sh_guard)
                        sl = slice(512 * h2, 512 * (h2 + 1))
                        for k in range(KB):
                            t.matmul(pgate[:, sl], wg_w[:, k, 128 * m:128 * (m + 1)],
                                     xnT[:, k, sl], start=(k == 0), stop=False)
                        for k in range(KB):
                            mm = t.matmul(pgate[:, sl], wg_u[:, k, 128 * m:128 * (m + 1)],
                                          c_final[:, k, sl], start=False, stop=(k == KB - 1))
                        t_wo.tag(mm)
                    op("tensor", mm_wo)
                    hslot = j % 2
                    vt1_g = v_t1.count - 1 if v_t1.count >= 2 else -1
                    sh_v = s_h.reserve()
                    def mk_h(s, m=m, h2=h2, hslot=hslot, two_v=two_v, vt1_g=vt1_g):
                        s.wait_ge(t_wo.h, two_v)
                        if vt1_g >= 0:
                            s.wait_ge(v_t1.h, vt1_g)
                        sl = slice(512 * h2, 512 * (h2 + 1))
                        s_h.tag(s.activation(hch[hslot][:], pgate[:, sl], AF.Tanh,
                                             bias=bias_t[:, m, 3:4]))
                    op("scalar", mk_h)
                    ttr_guard = t_tr.count
                    v_t1.reserve()
                    def mk_t1(v, m=m, h2=h2, j=j, hslot=hslot, sh_v=sh_v, gyb_v=gyb_v,
                              ttr_guard=ttr_guard, b=b):
                        v.wait_ge(s_h.h, sh_v)
                        v.wait_ge(g_yb.h, gyb_v)
                        if b > 0 and j == 0 and ttr_guard > 0:
                            v.wait_ge(t_tr.h, ttr_guard)
                        sl = slice(512 * h2, 512 * (h2 + 1))
                        v_t1.tag(v.tensor_add(t1[:, j, :], hch[hslot][:], ybT[:, m, sl]))
                    op("vector", mk_t1)
            vt1_v = v_t1.count
            for q in range(KB):
                jq = b * KB + q
                slot = q % 2
                vout_g = v_out.count - 1 if jq >= 2 else -1
                x2_v = x_ld.reserve(16)
                def ldx2(sy, b=b, q=q, slot=slot, vout_g=vout_g):
                    if vout_g >= 0:
                        sy.wait_ge(v_out.h, vout_g + 1)
                    x_ld.tag(sy.dma_start(out=xch2[slot][:], in_=x_in[b, 128 * q:128 * (q + 1), :]), 16)
                op("sync", ldx2)
                vo_serial = v_out.count
                ttr_v = t_tr.reserve()
                def tr_t1(t, q=q, vt1_v=vt1_v, vo_serial=vo_serial):
                    t.wait_ge(v_t1.h, vt1_v)
                    if vo_serial > 0:
                        t.wait_ge(v_out.h, vo_serial)
                    h2 = q // 4
                    qq = q % 4
                    for m in range(KB):
                        mm = t.transpose(pdt[:, 128 * m:128 * (m + 1)],
                                         t1[:, m * 2 + h2, 128 * qq:128 * (qq + 1)], eye_f[:])
                    t_tr.tag(mm)
                op("tensor", tr_t1)
                sout_g = st_out.count - 16 if jq >= 2 else -1
                vo_v = v_out.reserve()
                def mk_out(v, slot=slot, ttr_v=ttr_v, x2_v=x2_v, sout_g=sout_g):
                    v.wait_ge(t_tr.h, ttr_v)
                    v.wait_ge(x_ld.h, x2_v)
                    if sout_g >= 0:
                        v.wait_ge(st_out.h, sout_g + 16)
                    v_out.tag(v.tensor_add(outch[slot][:], xch2[slot][:], pdt[:]))
                op("vector", mk_out)
                st_out.reserve(16)
                def st_o(sy, b=b, q=q, slot=slot, vo_v=vo_v):
                    sy.wait_ge(v_out.h, vo_v)
                    st_out.tag(sy.dma_start(out=out_t[b, 128 * q:128 * (q + 1), :],
                                            in_=outch[slot][:]), 16)
                op("sync", st_o)
        FIN = st_out.count
        def fin_wait(sy):
            sy.wait_ge(st_out.h, FIN)
        op("sync", fin_wait)

    if DUMP:
        dmp1 = nc.dram_tensor("dump_ag1", [96 * NC, TK], BF16, kind="ExternalOutput").ap()
        dmp2 = nc.dram_tensor("dump_ag2", [D * NC, TK], BF16, kind="ExternalOutput").ap()
        dmp3 = nc.dram_tensor("dump_ag3", [DK * NC, L], BF16, kind="ExternalOutput").ap()
        dmp3b = nc.dram_tensor("dump_ag3b", [3, DK * NC, L], BF16, kind="ExternalOutput").ap()
        dmp1b = nc.dram_tensor("dump_ag1b", [3, 96 * NC, TK], BF16, kind="ExternalOutput").ap()
        dmp2b = nc.dram_tensor("dump_ag2b", [3, D * NC, TK], BF16, kind="ExternalOutput").ap()
        dmp2i = nc.dram_tensor("dump_ag2i", [B, D, TK], BF16, kind="ExternalOutput").ap()
        dmpxt = nc.dram_tensor("dump_xnts", [B, D, TK], BF16, kind="ExternalOutput").ap()
        dval = st_out.count
        st_out.reserve(320)
        def dump(g):
            g.wait_ge(st_out.h, dval)
            g.wait_ge(cc1.h, cc1.count)
            g.wait_ge(cc2.h, cc2.count)
            g.wait_ge(cc3.h, cc3.count)
            st_out.tag(g.dma_start(out=dmp1, in_=ag1o[0]), 16)
            st_out.tag(g.dma_start(out=dmp2, in_=ag2o[0]), 16)
            st_out.tag(g.dma_start(out=dmp3, in_=ag3o[0]), 16)
            for _bb in range(1, 4):
                st_out.tag(g.dma_start(out=dmp3b[_bb - 1], in_=ag3o[_bb]), 16)
                st_out.tag(g.dma_start(out=dmp1b[_bb - 1], in_=ag1o[_bb]), 16)
                st_out.tag(g.dma_start(out=dmp2b[_bb - 1], in_=ag2o[_bb]), 16)
            for _bb in range(B):
                st_out.tag(g.dma_start(out=dmp2i[_bb], in_=ag2i[_bb]), 16)
                st_out.tag(g.dma_start(out=dmpxt[_bb], in_=xnts[_bb]), 16)
        op("gpsimd", dump)
        DFIN = st_out.count
        def dump_wait(sy):
            sy.wait_ge(st_out.h, DFIN)
        op("sync", dump_wait)

    block = ctx.enter_context(nc.Block())

    @block.sync
    def _(sync):
        for eng, fn in ops:
            if eng == "sync":
                fn(sync)

    @block.gpsimd
    def _(gpsimd):
        for eng, fn in ops:
            if eng == "gpsimd":
                fn(gpsimd)

    @block.vector
    def _(vector):
        for eng, fn in ops:
            if eng == "vector":
                fn(vector)

    @block.scalar
    def _(scalar):
        for eng, fn in ops:
            if eng == "scalar":
                fn(scalar)

    @block.tensor
    def _(tensor):
        for eng, fn in ops:
            if eng == "tensor":
                fn(tensor)

    ctx.close()
    return nc


_NC_CACHE = {}


def kernel(L0_attn_x, L0_attn_W_xproj, L0_attn_W_dt, L0_attn_b_dt, L0_attn_A_log,
           L0_attn_D_param, L0_ffn_wf, L0_ffn_uf, L0_ffn_bf, L0_ffn_wi, L0_ffn_ui,
           L0_ffn_bi, L0_ffn_wc, L0_ffn_uc, L0_ffn_bc, L0_ffn_wo, L0_ffn_uo,
           L0_ffn_bo, L0_ffn_in_c):
    if "nc" not in _NC_CACHE:
        _NC_CACHE["nc"] = build_program()
    nc = _NC_CACHE["nc"]

    x = np.ascontiguousarray(np.asarray(L0_attn_x, dtype=np.float32))
    eye_f = np.eye(128, dtype=np.float32)
    eye_b = np.eye(128, dtype=ml_dtypes.bfloat16)
    sel = np.zeros((N, N, 128), dtype=ml_dtypes.bfloat16)
    for n in range(N):
        sel[n, n, :] = 1
    sel = np.ascontiguousarray(sel.reshape(N, N * 128))
    bias_pack = np.stack([np.asarray(v, dtype=np.float32).reshape(KB, 128).T
                          for v in (L0_ffn_bf, L0_ffn_bi, L0_ffn_bc, L0_ffn_bo, L0_ffn_in_c)],
                         axis=2)
    bias_pack = np.ascontiguousarray(bias_pack)
    wbig = [np.ascontiguousarray(np.asarray(w, dtype=np.float32).T) for w in
            (L0_ffn_wf, L0_ffn_uf, L0_ffn_wi, L0_ffn_ui, L0_ffn_wc, L0_ffn_uc,
             L0_ffn_wo, L0_ffn_uo)]
    wx = np.ascontiguousarray(np.asarray(L0_attn_W_xproj, dtype=np.float32).T)

    in_maps = []
    for c in range(NC):
        dsl = slice(DK * c, DK * (c + 1))
        m = {
            "x_in": np.ascontiguousarray(x[:, TK * c:TK * (c + 1), :]),
            "wx_in": wx,
            "wdt_in": np.ascontiguousarray(np.asarray(L0_attn_W_dt, dtype=np.float32)[dsl, :].T),
            "bdt_in": np.ascontiguousarray(np.asarray(L0_attn_b_dt, dtype=np.float32)[dsl].reshape(DK, 1)),
            "alog_in": np.ascontiguousarray(np.asarray(L0_attn_A_log, dtype=np.float32)[dsl, :]),
            "dvec_in": np.ascontiguousarray(np.asarray(L0_attn_D_param, dtype=np.float32)[dsl].reshape(DK, 1)),
            "bias_in": bias_pack,
            "eye_f_in": eye_f,
            "eye_b_in": eye_b,
            "sel_in": sel,
        }
        for g in range(8):
            m[f"w{g}_in"] = wbig[g]
        in_maps.append(m)

    res = run_bass_kernel_spmd(nc, in_maps, core_ids=list(range(NC)))
    return np.concatenate([np.asarray(r["out"]) for r in res.results], axis=1)


# version 10
# speedup vs baseline: 1.0123x; 1.0123x over previous
import numpy as np
import ml_dtypes
from contextlib import ExitStack

import concourse.bass as bass
import concourse.mybir as mybir
from concourse.bass_utils import run_bass_kernel_spmd

F32 = mybir.dt.float32
BF16 = mybir.dt.bfloat16
AF = mybir.ActivationFunctionType
ALU = mybir.AluOpType

NC = 8
B = 4
L = 8192
D = 1024
N = 16
R = 64
TK = L // NC
DK = D // NC
CH = 1024
NCH = L // CH
KB = D // 128


class Sem:

    def __init__(self, nc, name):
        self.h = nc.alloc_semaphore(name)
        self.count = 0

    def reserve(self, k=1):
        self.count += k
        return self.count

    def tag(self, instr, k=1):
        instr.then_inc(self.h, k)


STAGE = 3
DUMP = 0
SUB = 3
SUBY = 3


def build_program():
    nc = bass.Bass("TRN2", target_bir_lowering=False, debug=False, num_devices=NC)

    x_in = nc.dram_tensor("x_in", [B, TK, D], F32, kind="ExternalInput").ap()
    wx_in = nc.dram_tensor("wx_in", [D, 96], F32, kind="ExternalInput").ap()
    wdt_in = nc.dram_tensor("wdt_in", [R, DK], F32, kind="ExternalInput").ap()
    bdt_in = nc.dram_tensor("bdt_in", [DK, 1], F32, kind="ExternalInput").ap()
    alog_in = nc.dram_tensor("alog_in", [DK, N], F32, kind="ExternalInput").ap()
    dvec_in = nc.dram_tensor("dvec_in", [DK, 1], F32, kind="ExternalInput").ap()
    wbig_in = [nc.dram_tensor(f"w{g}_in", [D, D], F32, kind="ExternalInput").ap()
               for g in range(8)]
    bias_in = nc.dram_tensor("bias_in", [128, KB, 5], F32, kind="ExternalInput").ap()
    eye_f_in = nc.dram_tensor("eye_f_in", [128, 128], F32, kind="ExternalInput").ap()
    eye_b_in = nc.dram_tensor("eye_b_in", [128, 128], BF16, kind="ExternalInput").ap()
    sel_in = nc.dram_tensor("sel_in", [N, N * 128], BF16, kind="ExternalInput").ap()
    out_t = nc.dram_tensor("out", [B, TK, D], F32, kind="ExternalOutput").ap()

    ag1i = [nc.dram_tensor(f"ag1i_{b}", [96, TK], BF16).ap() for b in range(B)]
    ag1o = [nc.dram_tensor(f"ag1o_{b}", [96 * NC, TK], BF16).ap() for b in range(B)]
    ag2i = [nc.dram_tensor(f"ag2i_{b}", [D, TK], BF16).ap() for b in range(B)]
    ag2o = [nc.dram_tensor(f"ag2o_{b}", [D * NC, TK], BF16).ap() for b in range(B)]
    ag3i = [nc.dram_tensor(f"ag3i_{b}", [DK, L], BF16).ap() for b in range(B)]
    ag3o = [nc.dram_tensor(f"ag3o_{b}", [DK * NC, L], BF16).ap() for b in range(B)]
    xnts = nc.dram_tensor("xnts", [B, D, TK], BF16).ap()
    dump_dt = nc.dram_tensor("dump_dt", [B, 128, L], BF16, kind="ExternalOutput").ap() if DUMP else None
    dump_dtx = nc.dram_tensor("dump_dtx", [B, 128, L], BF16, kind="ExternalOutput").ap() if DUMP else None
    dump_br = nc.dram_tensor("dump_br", [128, CH], BF16, kind="ExternalOutput").ap() if DUMP else None
    dump_a = nc.dram_tensor("dump_a", [128, CH], F32, kind="ExternalOutput").ap() if DUMP else None
    dump_u = nc.dram_tensor("dump_u", [128, CH], BF16, kind="ExternalOutput").ap() if DUMP else None
    dump_h = nc.dram_tensor("dump_h", [128, CH], BF16, kind="ExternalOutput").ap() if DUMP else None
    dump_carry = nc.dram_tensor("dump_carry", [128, 16], F32, kind="ExternalOutput").ap() if DUMP else None
    dump_carry2 = nc.dram_tensor("dump_carry2", [128, 16], F32, kind="ExternalOutput").ap() if DUMP else None
    dump_cr = nc.dram_tensor("dump_cr", [128, CH], BF16, kind="ExternalOutput").ap() if DUMP else None
    dump_cc = nc.dram_tensor("dump_cc", [16, CH], BF16, kind="ExternalOutput").ap() if DUMP else None
    dump_xs = nc.dram_tensor("dump_xs", [128, CH], BF16, kind="ExternalOutput").ap() if DUMP else None
    if DUMP >= 5:
        dump_brall = nc.dram_tensor("dump_brall", [8, 16, 128, CH], BF16, kind="ExternalOutput").ap()
        dump_aall = nc.dram_tensor("dump_aall", [8, 16, 128, CH], F32, kind="ExternalOutput").ap()
        dump_uall = nc.dram_tensor("dump_uall", [8, 16, 128, CH], BF16, kind="ExternalOutput").ap()
        dump_hall = nc.dram_tensor("dump_hall", [8, 16, 128, CH], BF16, kind="ExternalOutput").ap()
        dump_crall = nc.dram_tensor("dump_crall", [8, 16, 128, CH], BF16, kind="ExternalOutput").ap()
        dump_pall = nc.dram_tensor("dump_pall", [8, 16, 128, CH], BF16, kind="ExternalOutput").ap()
        dump_yball = nc.dram_tensor("dump_yball", [8, 128, CH], BF16, kind="ExternalOutput").ap()
    wts = [nc.dram_tensor(f"wts_{g}", [D, D], BF16).ap() for g in range(8)]
    fence_d1 = nc.dram_tensor("fence_d1", [64, CH], BF16).ap()
    fence_d2 = nc.dram_tensor("fence_d2", [N, CH], BF16).ap()
    fence_d3 = nc.dram_tensor("fence_d3", [128, CH], BF16).ap()
    fence_d4 = nc.dram_tensor("fence_d4", [128, CH], BF16).ap()

    ctx = ExitStack()
    sb = lambda name, shape, dt: ctx.enter_context(nc.sbuf_tensor(name, shape, dt))

    wxT = sb("wxT", [128, KB, 96], BF16)
    wdtT = sb("wdtT", [64, 128], BF16)
    bdt_t = sb("bdt_t", [DK, 1], F32)
    alog_t = sb("alog_t", [DK, N], F32)
    a_mat = sb("a_mat", [DK, N], F32)
    dvec_t = sb("dvec_t", [DK, 1], F32)
    eye_f = sb("eye_f", [128, 128], F32)
    eye_b = sb("eye_b", [128, 128], BF16)
    sel_t = sb("sel_t", [N, N * 128], BF16)
    bias_t = sb("bias_t", [128, KB, 5], F32)

    dhi = sb("dhi", [128, 128], BF16)
    dlo = sb("dlo", [128, 128], BF16)

    wg_w = sb("wg_w", [128, KB, D], BF16)
    wg_u = sb("wg_u", [128, KB, D], BF16)

    xch = [sb(f"xch{i}", [128, D], F32) for i in range(2)]
    sqch = sb("sqch", [128, D], F32)
    xnf = sb("xnf", [128, D], F32)

    st_bn = sb("st_bn", [128, 12], F32)
    st_mv = sb("st_mv", [128, 2], F32)
    st_sum = sb("st_sum", [128, 1], F32)
    st_sq = sb("st_sq", [128, 1], F32)
    st_mean = sb("st_mean", [128, 1], F32)
    st_ex2 = sb("st_ex2", [128, 1], F32)
    st_m2 = sb("st_m2", [128, 1], F32)
    st_var = sb("st_var", [128, 1], F32)
    st_std = sb("st_std", [128, 1], F32)
    st_rstd = sb("st_rstd", [128, 1], F32)
    eps_t = sb("eps_t", [128, 1], F32)
    xnT = sb("xnT", [128, KB, TK], BF16)


    c_a = sb("c_a", [128, KB, TK], BF16)
    arena = sb("arena", [128, 8192], F32)
    xnbf = arena[:, 4096:4608].bitcast(BF16)
    xdbf = arena[0:96, 4608:5120].bitcast(BF16)
    ddiag = arena[:, 5120:5248]
    c_b = arena[:, 0:4096].bitcast(BF16).rearrange("p (k t) -> p k t", k=KB)
    gi_full = arena[:, 4096:8192].bitcast(BF16).rearrange("p (k t) -> p k t", k=KB)
    t1 = arena[:].rearrange("p (m t) -> p m t", m=16)
    gf_t = sb("gf_t", [128, 512], BF16)
    gg_t = sb("gg_t", [128, 512], BF16)
    cprod = sb("cprod", [128, 512], BF16)

    dtlo_r = [sb(f"dtlo{i}", [64, CH], BF16) for i in range(2)]
    ccomp_r = [sb(f"ccomp{i}", [N, CH], BF16) for i in range(2)]
    xnsl_r = [sb(f"xnsl{i}", [128, CH], BF16) for i in range(2)]
    dtbf_r = [sb(f"dtbf{i}", [128, CH], BF16) for i in range(2)]
    dtx_r = [sb(f"dtx{i}", [128, CH], BF16) for i in range(2)]
    et_t = sb("et_t", [128, 512], F32)
    dhi_f = et_t[:, 0:128]
    dlo_f = et_t[:, 128:256]
    a_ring = [sb(f"a_ring{i}", [128, CH], F32) for i in range(2)]
    brep = [sb(f"brep{i}", [128, CH], BF16) for i in range(5)]
    u_t = sb("u_t", [128, CH], BF16)
    h_t = sb("h_t", [128, CH], BF16)
    prod = [sb(f"prod{i}", [128, CH], BF16) for i in range(3)]
    crep = [sb(f"crep{i}", [128, CH], BF16) for i in range(2)]
    carry = sb("carry", [128, N], F32)
    ybf_r = [sb(f"ybf{i}", [128, CH], BF16) for i in range(2)]

    ybT = sb("ybT", [128, KB, TK], BF16)
    hch = [sb(f"hch{i}", [128, 512], F32) for i in range(2)]
    xch2 = xch
    outch = [xnf, sqch]

    pgate = ctx.enter_context(nc.psum_tensor("pgate", [128, 1024], F32))
    pdt = ctx.enter_context(nc.psum_tensor("pdt", [128, 1024], F32))
    pc = ctx.enter_context(nc.psum_tensor("pc", [128, 1024], F32))
    py = ctx.enter_context(nc.psum_tensor("py", [128, 1024], F32))

    Sm = lambda nm: Sem(nc, nm)
    x_ld = Sm("x_ld"); xt_ld = Sm("xt_ld"); misc_ld = Sm("misc_ld")
    w_ld = Sm("w_ld"); wt_tr = Sm("wt_tr"); wt_sp = Sm("wt_sp")
    v_xn = Sm("v_xn"); s_stat = Sm("s_stat"); v_stat = Sm("v_stat"); s_xnbf = Sm("s_xnbf")
    ag1w = Sm("ag1w"); ag2w = Sm("ag2w"); ag3w = Sm("ag3w"); xnts_w = Sm("xnts_w")
    cc1 = Sm("cc1"); cc2 = Sm("cc2"); cc3 = Sm("cc3")
    t_xd = Sm("t_xd"); s_xd = Sm("s_xd")
    g_ld = Sm("g_ld"); sc_ld = Sm("sc_ld")
    t_dt = Sm("t_dt"); s_dtexp = Sm("s_dtexp"); s_dt = Sm("s_dt"); v_dtx = Sm("v_dtx")
    g_brep = Sm("g_brep"); s_a = Sm("s_a"); v_u = Sm("v_u"); v_h = Sm("v_h")
    t_cr = Sm("t_cr"); s_cre = Sm("s_cre"); v_p = Sm("v_p"); t_p = Sm("t_p")
    t_yf = Sm("t_yf"); v_ye = Sm("v_ye"); t_D = Sm("t_D")
    t_gate = Sm("t_gate"); s_gate = Sm("s_gate"); v_c = Sm("v_c")
    wg_ld = Sm("wg_ld")
    t_wo = Sm("t_wo"); s_h = Sm("s_h"); v_t1 = Sm("v_t1"); g_yb = Sm("g_yb")
    t_tr = Sm("t_tr"); v_out = Sm("v_out"); st_out = Sm("st_out")
    s_prep = Sm("s_prep"); v_prep = Sm("v_prep"); v_drain = Sm("v_drain"); v_cz = Sm("v_cz"); d_dt = Sm("d_dt")
    v_ci = Sm("v_ci")
    t_xnt = Sm("t_xnt"); v_xnt = Sm("v_xnt")

    ops = []
    def op(eng, fn):
        ops.append((eng, fn))

    _dyn_cache = {}
    def dynslice(eng, size):
        key = size
        if key not in _dyn_cache:
            _dyn_cache[key] = bass.ts(eng.partition_id(), size)
        return _dyn_cache[key]

    RG = [list(range(NC))]
    gate_act_vals = []

    def ld_small(sc):
        for src, dst in ((bdt_in, bdt_t), (alog_in, alog_t), (dvec_in, dvec_t),
                         (eye_f_in, eye_f), (eye_b_in, eye_b), (sel_in, sel_t),
                         (bias_in, bias_t)):
            d = sc.dma_start(out=dst[:], in_=src)
            misc_ld.tag(d, 16)
    for _ in range(7):
        misc_ld.reserve(16)
    MV = misc_ld.count
    op("scalar", ld_small)

    def ld_cast0(g):
        w_ld.tag(g.dma_start(out=wxT[:], in_=wx_in.rearrange("(k p) c -> p k c", p=128)), 16)
        w_ld.tag(g.dma_start(out=wdtT[:], in_=wdt_in), 16)
    w_ld.reserve(32)
    W2 = w_ld.count
    op("gpsimd", ld_cast0)

    def prep_v(v):
        v_prep.tag(v.memset(eps_t[:], 1e-5))
    VP_MEMSET = v_prep.reserve()
    op("vector", prep_v)

    def prep_s(s):
        s.wait_ge(misc_ld.h, MV)
        s_prep.tag(s.activation(a_mat[:], alog_t[:], AF.Exp))
    SP0 = s_prep.reserve()
    op("scalar", prep_s)

    def prep_v2(v):
        v.wait_ge(s_prep.h, SP0)
        v.tensor_scalar(a_mat[:], a_mat[:], -1.0, None, ALU.mult)
        v_prep.tag(v.tensor_scalar(ddiag[:], eye_f[:], dvec_t[:], None, ALU.mult))
    VP0 = v_prep.reserve()
    op("vector", prep_v2)

    def prep_s2(s):
        s.wait_ge(v_prep.h, VP0)
        s.copy(dhi[:], ddiag[:])
        s_prep.tag(s.copy(dhi_f[:], dhi[:]))
    SP1 = s_prep.reserve()
    op("scalar", prep_s2)

    def prep_v3(v):
        v.wait_ge(s_prep.h, SP1)
        v_prep.tag(v.tensor_sub(dlo_f[:], ddiag[:], dhi_f[:]))
    VP1 = v_prep.reserve()
    op("vector", prep_v3)

    def prep_s3(s):
        s.wait_ge(v_prep.h, VP1)
        s_prep.tag(s.copy(dlo[:], dlo_f[:]))
    SPF = s_prep.reserve()
    op("scalar", prep_s3)

    WT_BASE = 0

    def weight_triplet(g):
        sp_guard = wt_sp.count
        def ld_w(gp, g=g, sp_guard=sp_guard, nb=g // 2 + 1):
            gp.wait_ge(cc2.h, nb)
            gp.wait_ge(cc1.h, nb)
            if sp_guard > 0:
                gp.wait_ge(wt_sp.h, sp_guard)
            w_ld.tag(gp.dma_start(out=wg_u[:], in_=wbig_in[g].rearrange("(k p) c -> p k c", p=128)), 16)
        wld_v = w_ld.reserve(16)
        op("gpsimd", ld_w)
        def sp_w(sc, g=g, wld_v=wld_v):
            sc.wait_ge(w_ld.h, wld_v)
            wt_sp.tag(sc.dma_start(out=wts[g].rearrange("(i p) c -> p i c", p=128), in_=wg_u[:]), 16)
        wt_sp.reserve(16)
        op("scalar", sp_w)

    for b in range(B):
        for q in range(KB):
            jq = b * KB + q
            slot = q % 2
            xc = xch[slot]
            vg = v_xn.count - 1 if jq >= 2 else -1
            xval = x_ld.reserve(16)
            def ldx(sy, b=b, q=q, xc=xc, vg=vg):
                if vg >= 0:
                    sy.wait_ge(v_xn.h, vg)
                x_ld.tag(sy.dma_start(out=xc[:], in_=x_in[b, 128 * q:128 * (q + 1), :]), 16)
            op("sync", ldx)
            sstd_guard = s_stat.count
            d1 = v_drain.reserve(); d2 = v_drain.reserve(); d3 = v_drain.reserve()
            vstat_v = v_stat.reserve()
            def stats_v(v, xc=xc, xval=xval, d1=d1, d2=d2, d3=d3,
                        sstd_guard=sstd_guard, jq_=jq):
                v.wait_ge(x_ld.h, xval)
                if sstd_guard > 0:
                    v.wait_ge(s_stat.h, sstd_guard)
                v_drain.tag(v.bn_stats(st_bn[:, 0:6], xc[:, 0:512]))
                v.wait_ge(v_drain.h, d1)
                v_drain.tag(v.bn_stats(st_bn[:, 6:12], xc[:, 512:1024]))
                v.wait_ge(v_drain.h, d2)
                v_drain.tag(v.bn_aggr(st_mv[:], st_bn[:]))
                v.wait_ge(v_drain.h, d3)
                v_stat.tag(v.tensor_copy(st_mean[:], st_mv[:, 0:1]))
            op("vector", stats_v)
            vxn_guard = v_xn.count
            sstd_v = s_stat.reserve()
            def std_s(s, vstat_v=vstat_v, vxn_guard=vxn_guard):
                s.wait_ge(v_stat.h, vstat_v)
                if vxn_guard > 0:
                    s.wait_ge(v_xn.h, vxn_guard)
                s.wait_ge(v_prep.h, 1)
                s.activation(st_std[:], st_mv[:, 1:2], AF.Ln, bias=eps_t[:])
                s_stat.tag(s.activation(st_rstd[:], st_std[:], AF.Exp, scale=-0.5))
            op("scalar", std_s)
            sbf_guard = s_xnbf.count
            vxn_v = v_xn.reserve()
            def norm_v(v, xc=xc, sstd_v=sstd_v, sbf_guard=sbf_guard):
                v.wait_ge(s_stat.h, sstd_v)
                if sbf_guard > 0:
                    v.wait_ge(s_xnbf.h, sbf_guard)
                v_xn.tag(v.tensor_scalar(xnf[:], xc[:], st_mean[:], st_rstd[:],
                                         ALU.subtract, ALU.mult))
            op("vector", norm_v)
            xt_guard = t_xnt.count
            sbf_v = s_xnbf.reserve()
            def tobf_s(s, vxn_v=vxn_v, xt_guard=xt_guard):
                s.wait_ge(v_xn.h, vxn_v)
                if xt_guard > 0:
                    s.wait_ge(t_xnt.h, xt_guard)
                s_xnbf.tag(s.copy(xnbf[:], xnf[:]))
            op("scalar", tobf_s)
            txd_guard = t_xd.count
            vxnt_guard = v_xnt.count
            txnt_v = t_xnt.reserve()
            def tr_xn(t, q=q, sbf_v=sbf_v, txd_guard=txd_guard, vxnt_guard=vxnt_guard):
                t.wait_ge(s_xnbf.h, sbf_v)
                if vxnt_guard > 0:
                    t.wait_ge(v_xnt.h, vxnt_guard)
                pcb = pc[:].bitcast(BF16)
                for k in range(KB):
                    mm = t.transpose(pcb[:, 128 * k:128 * (k + 1)],
                                     xnbf[:, 128 * k:128 * (k + 1)], eye_b[:])
                t_xnt.tag(mm)
            op("tensor", tr_xn)
            vxnt_v = v_xnt.reserve()
            def cp_xn(v, b=b, q=q, txnt_v=txnt_v, txd_guard=txd_guard):
                v.wait_ge(t_xnt.h, txnt_v)
                if q == 0 and txd_guard > 0:
                    v.wait_ge(t_xd.h, txd_guard)
                if q == 0 and b > 0:
                    v.wait_ge(ag2w.h, 16 * b)
                    v.wait_ge(xnts_w.h, 16 * b)
                v_xnt.tag(v.tensor_copy(xnT[:, :, 128 * q:128 * (q + 1)],
                                        pc[:].bitcast(BF16)[:, 0:1024].rearrange("p (k c) -> p k c", k=KB)))
            op("vector", cp_xn)
        xt_v = v_xnt.count
        ag2_v = ag2w.reserve(16)
        xnts_w.reserve(16)
        def wr_xn(sy, b=b, xt_v=xt_v):
            sy.wait_ge(v_xnt.h, xt_v)
            ag2w.tag(sy.dma_start(out=ag2i[b].rearrange("(k p) t -> p k t", p=128), in_=xnT[:]), 16)
            xnts_w.tag(sy.dma_start(out=xnts[b].rearrange("(k p) t -> p k t", p=128), in_=xnT[:]), 16)
        op("scalar", wr_xn)
        cc2.reserve(1)
        def coll2(g, b=b, ag2_v=ag2_v, wld_all=w_ld.count, sp_all=wt_sp.count):
            g.wait_ge(ag2w.h, ag2_v)
            g.wait_ge(w_ld.h, wld_all)
            if sp_all > 0:
                g.wait_ge(wt_sp.h, sp_all)
            if b > 0:
                g.wait_ge(cc1.h, b)
            cc2.tag(g.collective_compute("AllGather", ALU.bypass, replica_groups=RG,
                                         ins=[ag2i[b]], outs=[ag2o[b]]))
        op("gpsimd", coll2)
        for h2 in range(2):
            sxd_g = s_xd.count
            txd_v = t_xd.reserve()
            first_t = (b == 0 and h2 == 0)
            def mm_xd(t, h2=h2, xt_v=xt_v, sxd_g=sxd_g, first_t=first_t):
                if first_t:
                    t.wait_ge(misc_ld.h, MV)
                    t.wait_ge(s_prep.h, SPF)
                t.wait_ge(v_xnt.h, xt_v)
                t.wait_ge(w_ld.h, W2)
                if sxd_g > 0:
                    t.wait_ge(s_xd.h, sxd_g)
                sl = slice(512 * h2, 512 * (h2 + 1))
                for k in range(KB):
                    mm = t.matmul(pdt[0:96, sl], wxT[:, k, :], xnT[:, k, sl],
                                  start=(k == 0), stop=(k == KB - 1))
                t_xd.tag(mm)
            op("tensor", mm_xd)
            s_xd.reserve()
            def ev_xd(s, b=b, h2=h2, txd_v=txd_v):
                s.wait_ge(t_xd.h, txd_v)
                if h2 == 0 and b > 0:
                    s.wait_ge(ag1w.h, 16 * b)
                sl = slice(512 * h2, 512 * (h2 + 1))
                s_xd.tag(s.copy(xdbf[:, sl], pdt[0:96, sl]))
            op("scalar", ev_xd)
        sxd_v = s_xd.count
        ag1_v = ag1w.reserve(16)
        def wr_xd(g, b=b, sxd_v=sxd_v):
            g.wait_ge(s_xd.h, sxd_v)
            ag1w.tag(g.dma_start(out=ag1i[b], in_=xdbf[:]), 16)
        op("scalar", wr_xd)
        cc1.reserve(1)
        def coll1(g, b=b, ag1_v=ag1_v, wld_all=w_ld.count, sp_all=wt_sp.count):
            g.wait_ge(ag1w.h, ag1_v)
            g.wait_ge(w_ld.h, wld_all)
            if sp_all > 0:
                g.wait_ge(wt_sp.h, sp_all)
            g.wait_ge(cc2.h, b + 1)
            cc1.tag(g.collective_compute("AllGather", ALU.bypass, replica_groups=RG,
                                         ins=[ag1i[b]], outs=[ag1o[b]]))
        op("gpsimd", coll1)
        weight_triplet(2 * b)
        weight_triplet(2 * b + 1)


    X_DBL_DONE = t_xd.count
    P1_AG2 = ag2w.count
    P1_XNTS = xnts_w.count
    P1_XNT = v_xnt.count
    ALL_SPILLS = wt_sp.count

    def make_ffn_step(b):
        c_old = c_a if b % 2 == 0 else c_b
        c_new = c_b if b % 2 == 0 else c_a
        units = []
        tgate_guard = t_gate.count
        xn_v = sc_ld.reserve(16)
        def ld_xn(sc, b=b, tgate_guard=tgate_guard):
            if b == 0:
                sc.wait_ge(t_xd.h, X_DBL_DONE)
                sc.wait_ge(ag2w.h, P1_AG2)
                sc.wait_ge(xnts_w.h, P1_XNTS)
            else:
                sc.wait_ge(t_gate.h, tgate_guard)
            sc.wait_ge(xnts_w.h, 16 * (b + 1))
            sc_ld.tag(sc.dma_start(out=xnT[:], in_=xnts[b].rearrange("(k p) t -> p k t", p=128)), 16)
        units.append(("scalar", ld_xn))
        cinit_v = -1
        if b == 0:
            cinit_v = v_c.reserve()
            vci_v = v_ci.reserve()
            def init_c(v, vci_v=vci_v):
                v.wait_ge(misc_ld.h, MV)
                v_ci.tag(v.memset(c_a[:], 0.0))
                v.wait_ge(v_ci.h, vci_v)
                for k in range(KB - 1):
                    v.tensor_scalar(c_a[:, k, :], c_a[:, k, :], bias_t[:, k, 4:5],
                                    None, ALU.add)
                k = KB - 1
                v_c.tag(v.tensor_scalar(c_a[:, k, :], c_a[:, k, :], bias_t[:, k, 4:5],
                                        None, ALU.add))
            units.append(("vector", init_c))
        for gi in range(3):
            func_is_sig = gi < 2
            bj = gi
            wld_guard = t_gate.count
            wgl_v = wg_ld.reserve(32)
            def ld_wg(sc, gi=gi, wld_guard=wld_guard):
                if wld_guard > 0:
                    sc.wait_ge(t_gate.h, wld_guard)
                sc.wait_ge(wt_sp.h, ALL_SPILLS)
                wg_ld.tag(sc.dma_start(out=wg_w[:], in_=wts[2 * gi].rearrange("(k p) c -> p k c", p=128)), 16)
                wg_ld.tag(sc.dma_start(out=wg_u[:], in_=wts[2 * gi + 1].rearrange("(k p) c -> p k c", p=128)), 16)
            units.append(("scalar", ld_wg))
            for m in range(KB):
                for h2 in range(2):
                    uglobal = len(gate_act_vals)
                    half_guard = gate_act_vals[uglobal - 2] if uglobal >= 2 else -1
                    vc_guard = v_c.count
                    tg_v = t_gate.reserve()
                    def mm_g(t, gi=gi, m=m, h2=h2, b=b, wgl_v=wgl_v, xn_v=xn_v,
                             half_guard=half_guard, c_old=c_old, vc_guard=vc_guard,
                             cinit_v=cinit_v):
                        t.wait_ge(wg_ld.h, wgl_v)
                        t.wait_ge(sc_ld.h, xn_v)
                        if half_guard >= 0:
                            t.wait_ge(s_gate.h, half_guard)
                        if b == 0:
                            t.wait_ge(v_c.h, cinit_v)
                        elif vc_guard > 0:
                            t.wait_ge(v_c.h, vc_guard)
                        sl = slice(512 * h2, 512 * (h2 + 1))
                        for k in range(KB):
                            t.matmul(pgate[:, sl], wg_w[:, k, 128 * m:128 * (m + 1)],
                                     xnT[:, k, sl], start=(k == 0), stop=False)
                        for k in range(KB):
                            mm = t.matmul(pgate[:, sl], wg_u[:, k, 128 * m:128 * (m + 1)],
                                          c_old[:, k, sl], start=False, stop=(k == KB - 1))
                        t_gate.tag(mm)
                    units.append(("tensor", mm_g))
                    vc_g2 = v_c.count
                    sg_v = s_gate.reserve()
                    gate_act_vals.append(sg_v)
                    def act_g(s, gi=gi, m=m, h2=h2, bj=bj, tg_v=tg_v, vc_g2=vc_g2,
                              func_is_sig=func_is_sig):
                        s.wait_ge(t_gate.h, tg_v)
                        if gi in (0, 2) and vc_g2 > 0:
                            s.wait_ge(v_c.h, vc_g2)
                        sl = slice(512 * h2, 512 * (h2 + 1))
                        if gi == 0:
                            dst = gf_t[:]
                        elif gi == 1:
                            dst = gi_full[:, m, sl]
                        else:
                            dst = gg_t[:]
                        s_gate.tag(s.activation(dst, pgate[:, sl],
                                                AF.Sigmoid if func_is_sig else AF.Tanh,
                                                bias=bias_t[:, m, bj:bj + 1]))
                    units.append(("scalar", act_g))
                    if gi == 0:
                        v_c.reserve()
                        def upd_f(v, m=m, h2=h2, sg_v=sg_v, c_old=c_old, c_new=c_new):
                            v.wait_ge(s_gate.h, sg_v)
                            sl = slice(512 * h2, 512 * (h2 + 1))
                            v_c.tag(v.tensor_mul(c_new[:, m, sl], gf_t[:], c_old[:, m, sl]))
                        units.append(("vector", upd_f))
                    elif gi == 2:
                        v_c.reserve()
                        def upd_g(v, m=m, h2=h2, sg_v=sg_v, c_new=c_new):
                            v.wait_ge(s_gate.h, sg_v)
                            sl = slice(512 * h2, 512 * (h2 + 1))
                            v.tensor_mul(cprod[:], gi_full[:, m, sl], gg_t[:])
                            v_c.tag(v.tensor_add(c_new[:, m, sl], c_new[:, m, sl], cprod[:]))
                        units.append(("vector", upd_g))
        return units

    for b in (range(B) if STAGE >= 2 else []):
        cc1_v = b + 1
        cc2_v = b + 1
        ffn_units = make_ffn_step(b)
        fu_iter = iter(ffn_units)
        fu_per_chunk = ((len(ffn_units) + NCH - 1) // NCH) if SUB >= 3 else 0

        vcz_v = v_cz.reserve()
        def zero_carry(v, b=b, vcz_v=vcz_v):
            v_cz.tag(v.memset(carry[:], 0.0))
            v.wait_ge(v_cz.h, vcz_v)
        op("vector", zero_carry)
        for c8 in range(NCH):
            it_base = b * NCH + c8
            rslot = c8 % 2
            tdt_guard = t_dt.count - 2 if it_base >= 2 else -1
            tcr_guard = t_cr.count - 16 if it_base >= 2 else -1
            vdtx_guard = v_dtx.count - 1 if it_base >= 2 else -1
            tD_guard = t_D.count - 1 if it_base >= 2 else -1
            gld_v = g_ld.reserve(96)
            ddt_g = d_dt.count
            def ld_chunk(g, b=b, c8=c8, rslot=rslot, cc1_v=cc1_v,
                         tdt_guard=tdt_guard, tcr_guard=tcr_guard, ddt_g=ddt_g):
                g.wait_ge(cc1.h, cc1_v)
                if tdt_guard >= 0:
                    g.wait_ge(t_dt.h, tdt_guard)
                if tcr_guard >= 0:
                    g.wait_ge(t_cr.h, tcr_guard)
                if DUMP and ddt_g > 0:
                    g.wait_ge(d_dt.h, ddt_g)
                g_ld.tag(g.dma_start(out=dtlo_r[rslot][:],
                                     in_=ag1o[b][96 * c8:96 * c8 + 64, :]), 16)
                g_ld.tag(g.dma_start(out=ccomp_r[rslot][:],
                                     in_=ag1o[b][96 * c8 + 80:96 * (c8 + 1), :]), 16)
            op("sync", ld_chunk)
            def ld_xnsl(sy, b=b, c8=c8, rslot=rslot, cc2_v=cc2_v,
                        vdtx_guard=vdtx_guard, tD_guard=tD_guard):
                sy.wait_ge(cc2.h, cc2_v)
                if vdtx_guard >= 0:
                    sy.wait_ge(v_dtx.h, vdtx_guard)
                if tD_guard >= 0:
                    sy.wait_ge(t_D.h, tD_guard)
                src2 = ag2o[b].rearrange("(r z) t -> z r t", z=D)[:, c8, :]
                g_ld.tag(sy.dma_start(out=xnsl_r[rslot][:],
                                      in_=src2[dynslice(sy, 128), :]), 16)
                g_ld.tag(sy.dma_start(out=fence_d1, in_=dtlo_r[rslot][:]), 16)
                g_ld.tag(sy.dma_start(out=fence_d2, in_=ccomp_r[rslot][:]), 16)
                g_ld.tag(sy.dma_start(out=fence_d3, in_=xnsl_r[rslot][:]), 16)
            op("sync", ld_xnsl)
            for half in range(2):
                uu = it_base * 2 + half
                sdte_g = s_dtexp.count - 1 if uu >= 2 else -1
                tdt_v = t_dt.reserve()
                def mm_dt(t, rslot=rslot, half=half, gld_v=gld_v, sdte_g=sdte_g):
                    t.wait_ge(g_ld.h, gld_v)
                    t.wait_ge(s_xd.h, X_DBL_DONE)
                    if sdte_g >= 0:
                        t.wait_ge(s_dtexp.h, sdte_g)
                    sl = slice(512 * half, 512 * (half + 1))
                    t_dt.tag(t.matmul(pdt[:, sl], wdtT[0:64, :], dtlo_r[rslot][:, sl],
                                      start=True, stop=True))
                op("tensor", mm_dt)
                vdtx_g2 = v_dtx.count - 1 if it_base >= 2 else -1
                s_dtexp.reserve()
                sdt_half = s_dt.reserve()
                def sp_dt(s, rslot=rslot, half=half, tdt_v=tdt_v, vdtx_g2=vdtx_g2):
                    s.wait_ge(t_dt.h, tdt_v)
                    s.wait_ge(misc_ld.h, MV)
                    if half == 0 and vdtx_g2 >= 0:
                        s.wait_ge(v_dtx.h, vdtx_g2)
                    sl = slice(512 * half, 512 * (half + 1))
                    s_dtexp.tag(s.activation(et_t[:], pdt[:, sl], AF.Exp, bias=bdt_t[:]))
                    s_dt.tag(s.activation(dtbf_r[rslot][:, sl], et_t[:], AF.Ln, bias=1.0))
                op("scalar", sp_dt)
            sdt_v = s_dt.count
            if DUMP:
                d_dt.reserve(16)
                def dmp_dt(sy, b=b, c8=c8, rslot=rslot, sdt_v=sdt_v):
                    sy.wait_ge(s_dt.h, sdt_v)
                    d_dt.tag(sy.dma_start(out=dump_dt[b][:, CH * c8:CH * (c8 + 1)],
                                          in_=dtbf_r[rslot][:]), 16)
                op("sync", dmp_dt)
            vdtx_v = v_dtx.reserve()
            def mk_dtx(v, rslot=rslot, sdt_v=sdt_v, gld_v=gld_v):
                v.wait_ge(s_dt.h, sdt_v)
                v.wait_ge(g_ld.h, gld_v)
                v_dtx.tag(v.tensor_mul(dtx_r[rslot][:], dtbf_r[rslot][:], xnsl_r[rslot][:]))
            op("vector", mk_dtx)
            if DUMP:
                d_dt.reserve(16)
                def dmp_dtx(sy, b=b, c8=c8, rslot=rslot, vdtx_v=vdtx_v):
                    sy.wait_ge(v_dtx.h, vdtx_v)
                    d_dt.tag(sy.dma_start(out=dump_dtx[b][:, CH * c8:CH * (c8 + 1)],
                                          in_=dtx_r[rslot][:]), 16)
                op("sync", dmp_dtx)
            vye_g = v_ye.count
            t_D.reserve()
            def mm_D(t, rslot=rslot, vye_g=vye_g, gld_v=gld_v):
                if vye_g > 0:
                    t.wait_ge(v_ye.h, vye_g)
                t.wait_ge(g_ld.h, gld_v)
                for half in range(2):
                    sl = slice(512 * half, 512 * (half + 1))
                    t.matmul(py[:, sl], dhi[:], xnsl_r[rslot][:, sl], start=True, stop=False)
                    mm = t.matmul(py[:, sl], dlo[:], xnsl_r[rslot][:, sl], start=False, stop=False)
                t_D.tag(mm)
            op("tensor", mm_D)
            dgate = 0
            for n in (range(N) if SUB >= 1 else []):
                it = it_base * N + n
                bslot = it % 5
                vu_g = v_u.count - 4 if it >= 5 else -1
                gb_v = g_brep.reserve(16)
                def bro(g, b=b, c8=c8, n=n, bslot=bslot, vu_g=vu_g, cc1_v=cc1_v, dg=dgate):
                    g.wait_ge(cc1.h, cc1_v)
                    if dg > 0:
                        g.wait_ge(d_dt.h, dg)
                    if vu_g >= 0:
                        g.wait_ge(v_u.h, vu_g)
                    row = 96 * c8 + 64 + n
                    g_brep.tag(g.dma_start(out=brep[bslot][:],
                                           in_=ag1o[b][row:row + 1, :].partition_broadcast(128)), 16)
                op("sync", bro)
                if n == N - 1:
                    g_brep.reserve(16)
                    def bro_fence(sy, bslot=bslot):
                        g_brep.tag(sy.dma_start(out=fence_d4, in_=brep[bslot][:]), 16)
                    op("sync", bro_fence)
                aslot = it % 2
                vh_g = v_h.count - 1 if it >= 2 else -1
                sa_v = s_a.reserve()
                def mk_a(s, n=n, rslot=rslot, aslot=aslot, vh_g=vh_g, sdt_v=sdt_v, dg=dgate):
                    s.wait_ge(s_dt.h, sdt_v)
                    if dg > 0:
                        s.wait_ge(d_dt.h, dg)
                    if vh_g >= 0:
                        s.wait_ge(v_h.h, vh_g)
                    s_a.tag(s.activation(a_ring[aslot][:], dtbf_r[rslot][:], AF.Exp,
                                         scale=a_mat[:, n:n + 1]))
                op("scalar", mk_a)
                v_u.reserve()
                vh_v = v_h.reserve()
                def scan_v(v, n=n, rslot=rslot, bslot=bslot, aslot=aslot, gb_v=gb_v,
                           sa_v=sa_v, vdtx_v=vdtx_v, vh_v=vh_v, vcz_v=vcz_v, first=(c8 == 0),
                           dg=dgate):
                    if dg > 0:
                        v.wait_ge(d_dt.h, dg)
                    if first:
                        v.wait_ge(v_cz.h, vcz_v)
                    v.wait_ge(v_dtx.h, vdtx_v)
                    v.wait_ge(g_brep.h, gb_v + (0 if DUMP else 16))
                    v_u.tag(v.tensor_mul(u_t[:], dtx_r[rslot][:], brep[bslot][:]))
                    v.wait_ge(s_a.h, sa_v)
                    v_h.tag(v.tensor_tensor_scan(h_t[:], a_ring[aslot][:], u_t[:],
                                                 carry[:, n:n + 1], ALU.mult, ALU.add))
                    v.wait_ge(v_h.h, vh_v)
                    v.tensor_copy(carry[:, n:n + 1], h_t[:, CH - 1:CH])
                op("vector", scan_v)
                scre_g = s_cre.count if it >= 1 else -1
                tcr_v = t_cr.reserve()
                first_cr = (it == 0 and b == 0)
                def mk_cr(t, n=n, rslot=rslot, gld_v=gld_v, scre_g=scre_g, first_cr=first_cr):
                    t.wait_ge(g_ld.h, gld_v)
                    if first_cr:
                        t.wait_ge(v_xnt.h, P1_XNT)
                    if scre_g >= 0:
                        t.wait_ge(s_cre.h, scre_g)
                    for half in range(2):
                        sl = slice(512 * half, 512 * (half + 1))
                        mm = t.matmul(pc[:, sl], sel_t[:, 128 * n:128 * (n + 1)],
                                      ccomp_r[rslot][:, sl], start=True, stop=True)
                    t_cr.tag(mm)
                op("tensor", mk_cr)
                cslot = it % 2
                vp_g = v_p.count - 1 if it >= 2 else -1
                scre_v = s_cre.reserve()
                def ev_cr(s, cslot=cslot, tcr_v=tcr_v, vp_g=vp_g, dg=dgate):
                    s.wait_ge(t_cr.h, tcr_v)
                    if dg > 0:
                        s.wait_ge(d_dt.h, dg)
                    if vp_g >= 0:
                        s.wait_ge(v_p.h, vp_g)
                    s_cre.tag(s.copy(crep[cslot][:], pc[:]))
                op("scalar", ev_cr)
                pslot = it % 3
                tp_g = t_p.count - 2 if it >= 3 else -1
                vp_v = v_p.reserve()
                def mk_p(v, pslot=pslot, cslot=cslot, scre_v=scre_v, tp_g=tp_g, dg=dgate):
                    v.wait_ge(s_cre.h, scre_v)
                    if dg > 0:
                        v.wait_ge(d_dt.h, dg)
                    if tp_g >= 0:
                        v.wait_ge(t_p.h, tp_g)
                    v_p.tag(v.tensor_mul(prod[pslot][:], h_t[:], crep[cslot][:]))
                op("vector", mk_p)
                if DUMP and b == 0 and c8 == 0 and n == 0:
                    d_dt.reserve(16 * 8)
                    def dmp_sc(sy, bslot=bslot, aslot=aslot, cslot=cslot, rslot=rslot,
                               vh_v=vh_v, scre_v=scre_v, vp_v=vp_v):
                        sy.wait_ge(v_h.h, vh_v)
                        sy.wait_ge(s_cre.h, scre_v)
                        sy.wait_ge(v_p.h, vp_v)
                        d_dt.tag(sy.dma_start(out=dump_br, in_=brep[bslot][:]), 16)
                        d_dt.tag(sy.dma_start(out=dump_a, in_=a_ring[aslot][:]), 16)
                        d_dt.tag(sy.dma_start(out=dump_u, in_=u_t[:]), 16)
                        d_dt.tag(sy.dma_start(out=dump_h, in_=h_t[:]), 16)
                        d_dt.tag(sy.dma_start(out=dump_carry, in_=carry[:]), 16)
                        d_dt.tag(sy.dma_start(out=dump_cr, in_=crep[cslot][:]), 16)
                        d_dt.tag(sy.dma_start(out=dump_cc, in_=ccomp_r[rslot][:]), 16)
                        d_dt.tag(sy.dma_start(out=dump_xs, in_=xnsl_r[rslot][:]), 16)
                    op("sync", dmp_sc)
                    dgate = d_dt.count
                if DUMP >= 5 and b == 0:
                    d_dt.reserve(16 * 6)
                    def dmp_all(sy, c8=c8, n=n, bslot=bslot, aslot=aslot, cslot=cslot,
                                pslot=pslot, vh_v=vh_v, scre_v=scre_v, vp_v=vp_v):
                        sy.wait_ge(v_h.h, vh_v)
                        sy.wait_ge(s_cre.h, scre_v)
                        sy.wait_ge(v_p.h, vp_v)
                        d_dt.tag(sy.dma_start(out=dump_brall[c8, n], in_=brep[bslot][:]), 16)
                        d_dt.tag(sy.dma_start(out=dump_aall[c8, n], in_=a_ring[aslot][:]), 16)
                        d_dt.tag(sy.dma_start(out=dump_uall[c8, n], in_=u_t[:]), 16)
                        d_dt.tag(sy.dma_start(out=dump_hall[c8, n], in_=h_t[:]), 16)
                        d_dt.tag(sy.dma_start(out=dump_crall[c8, n], in_=crep[cslot][:]), 16)
                        d_dt.tag(sy.dma_start(out=dump_pall[c8, n], in_=prod[pslot][:]), 16)
                    op("sync", dmp_all)
                    dgate = d_dt.count
                if DUMP and b == 0 and c8 == 0 and n == N - 1:
                    d_dt.reserve(16)
                    def dmp_cy2(sy, vp_v=vp_v):
                        sy.wait_ge(v_p.h, vp_v)
                        d_dt.tag(sy.dma_start(out=dump_carry2, in_=carry[:]), 16)
                    op("sync", dmp_cy2)
                    dgate = d_dt.count
                if SUB < 2 or SUBY < 1:
                    continue
                t_p.reserve()
                def mk_y(t, n=n, pslot=pslot, vp_v=vp_v):
                    t.wait_ge(v_p.h, vp_v)
                    for half in range(2):
                        sl = slice(512 * half, 512 * (half + 1))
                        mm = t.matmul(py[:, sl], eye_b[:], prod[pslot][:, sl],
                                      start=False, stop=(n == N - 1))
                    t_p.tag(mm)
                op("tensor", mk_y)
            if SUB < 2 or SUBY < 2:
                continue
            tyf_v = t_p.count
            yslot = c8 % 2
            ag3_guard = ag3w.count - 16 if it_base >= 2 else -1
            vye_v = v_ye.reserve()
            def ev_y(v, yslot=yslot, tyf_v=tyf_v, ag3_guard=ag3_guard, dg=dgate):
                v.wait_ge(t_p.h, tyf_v)
                if dg > 0:
                    v.wait_ge(d_dt.h, dg)
                if ag3_guard >= 0:
                    v.wait_ge(ag3w.h, ag3_guard)
                v_ye.tag(v.tensor_copy(ybf_r[yslot][:], py[:]))
            op("vector", ev_y)
            if DUMP >= 5 and b == 0:
                d_dt.reserve(16)
                def dmp_ybf(sy, c8=c8, yslot=yslot, vye_v=vye_v):
                    sy.wait_ge(v_ye.h, vye_v)
                    d_dt.tag(sy.dma_start(out=dump_yball[c8], in_=ybf_r[yslot][:]), 16)
                op("sync", dmp_ybf)
                dgate = d_dt.count
            ag3w.reserve(16)
            def wr_y(sy, b=b, c8=c8, yslot=yslot, vye_v=vye_v):
                sy.wait_ge(v_ye.h, vye_v)
                ag3w.tag(sy.dma_start(out=ag3i[b][:, CH * c8:CH * (c8 + 1)],
                                      in_=ybf_r[yslot][:]), 16)
            op("scalar", wr_y)
            for _ in range(fu_per_chunk):
                try:
                    ops.append(next(fu_iter))
                except StopIteration:
                    break
        if SUB >= 3:
            for rest in fu_iter:
                ops.append(rest)
        if SUB < 2 or SUBY < 3:
            continue
        ag3_v = ag3w.count
        cc3.reserve(1)
        def coll3(g, b=b, ag3_v=ag3_v, wld_all=w_ld.count, sp_all=wt_sp.count):
            g.wait_ge(ag3w.h, ag3_v)
            if b == 0:
                g.wait_ge(cc2.h, B)
                g.wait_ge(cc1.h, B)
                g.wait_ge(w_ld.h, wld_all)
                g.wait_ge(wt_sp.h, sp_all)
            else:
                g.wait_ge(cc3.h, b)
            cc3.tag(g.collective_compute("AllGather", ALU.bypass, replica_groups=RG,
                                         ins=[ag3i[b]], outs=[ag3o[b]]))
        op("gpsimd", coll3)

    if STAGE < 3:
        def fin_stub(g):
            g.wait_ge(cc2.h, cc2.count)
            g.wait_ge(cc1.h, cc1.count)
            if STAGE >= 2:
                g.wait_ge(cc3.h, cc3.count)
        op("gpsimd", fin_stub)
    c_final = c_a if B % 2 == 0 else c_b
    if STAGE >= 3:
        TG_FINAL = t_gate.count
        wo_v = wg_ld.reserve(32)
        def ld_wo(sc):
            sc.wait_ge(t_gate.h, TG_FINAL)
            sc.wait_ge(wt_sp.h, ALL_SPILLS)
            wg_ld.tag(sc.dma_start(out=wg_w[:], in_=wts[6].rearrange("(k p) c -> p k c", p=128)), 16)
            wg_ld.tag(sc.dma_start(out=wg_u[:], in_=wts[7].rearrange("(k p) c -> p k c", p=128)), 16)
        op("scalar", ld_wo)
        VC_FINAL = v_c.count
        for b in range(B):
            two_guard = t_wo.count
            xn3_v = sc_ld.reserve(16)
            def ld_xn3(sc, b=b, two_guard=two_guard):
                if two_guard > 0:
                    sc.wait_ge(t_wo.h, two_guard)
                else:
                    sc.wait_ge(t_gate.h, TG_FINAL)
                sc_ld.tag(sc.dma_start(out=xnT[:], in_=xnts[b].rearrange("(k p) t -> p k t", p=128)), 16)
            op("scalar", ld_xn3)
            vt1_guard = v_t1.count
            gyb_v = g_yb.reserve(16)
            def ld_yb(sy, b=b, vt1_guard=vt1_guard):
                sy.wait_ge(cc3.h, b + 1)
                if vt1_guard > 0:
                    sy.wait_ge(v_t1.h, vt1_guard)
                g_yb.tag(sy.dma_start(out=ybT[:],
                                      in_=ag3o[b].rearrange("(k p) t -> p k t", p=128)[:, :, dynslice(sy, TK)]), 16)
            op("sync", ld_yb)
            for m in range(KB):
                for h2 in range(2):
                    j = m * 2 + h2
                    sh_guard = s_h.count - 1 if s_h.count >= 1 else -1
                    two_v = t_wo.reserve()
                    def mm_wo(t, m=m, h2=h2, sh_guard=sh_guard, c_final=c_final,
                              xn3_v=xn3_v):
                        t.wait_ge(wg_ld.h, wo_v)
                        t.wait_ge(sc_ld.h, xn3_v)
                        t.wait_ge(v_c.h, VC_FINAL)
                        if sh_guard >= 1:
                            t.wait_ge(s_h.h, sh_guard)
                        sl = slice(512 * h2, 512 * (h2 + 1))
                        for k in range(KB):
                            t.matmul(pgate[:, sl], wg_w[:, k, 128 * m:128 * (m + 1)],
                                     xnT[:, k, sl], start=(k == 0), stop=False)
                        for k in range(KB):
                            mm = t.matmul(pgate[:, sl], wg_u[:, k, 128 * m:128 * (m + 1)],
                                          c_final[:, k, sl], start=False, stop=(k == KB - 1))
                        t_wo.tag(mm)
                    op("tensor", mm_wo)
                    hslot = j % 2
                    vt1_g = v_t1.count - 1 if v_t1.count >= 2 else -1
                    sh_v = s_h.reserve()
                    def mk_h(s, m=m, h2=h2, hslot=hslot, two_v=two_v, vt1_g=vt1_g):
                        s.wait_ge(t_wo.h, two_v)
                        if vt1_g >= 0:
                            s.wait_ge(v_t1.h, vt1_g)
                        sl = slice(512 * h2, 512 * (h2 + 1))
                        s_h.tag(s.activation(hch[hslot][:], pgate[:, sl], AF.Tanh,
                                             bias=bias_t[:, m, 3:4]))
                    op("scalar", mk_h)
                    ttr_guard = t_tr.count
                    v_t1.reserve()
                    def mk_t1(v, m=m, h2=h2, j=j, hslot=hslot, sh_v=sh_v, gyb_v=gyb_v,
                              ttr_guard=ttr_guard, b=b):
                        v.wait_ge(s_h.h, sh_v)
                        v.wait_ge(g_yb.h, gyb_v)
                        if b > 0 and j == 0 and ttr_guard > 0:
                            v.wait_ge(t_tr.h, ttr_guard)
                        sl = slice(512 * h2, 512 * (h2 + 1))
                        v_t1.tag(v.tensor_add(t1[:, j, :], hch[hslot][:], ybT[:, m, sl]))
                    op("vector", mk_t1)
            vt1_v = v_t1.count
            for q in range(KB):
                jq = b * KB + q
                slot = q % 2
                vout_g = v_out.count - 1 if jq >= 2 else -1
                x2_v = x_ld.reserve(16)
                def ldx2(sy, b=b, q=q, slot=slot, vout_g=vout_g):
                    if vout_g >= 0:
                        sy.wait_ge(v_out.h, vout_g + 1)
                    x_ld.tag(sy.dma_start(out=xch2[slot][:], in_=x_in[b, 128 * q:128 * (q + 1), :]), 16)
                op("sync", ldx2)
                vo_serial = v_out.count
                ttr_v = t_tr.reserve()
                def tr_t1(t, q=q, vt1_v=vt1_v, vo_serial=vo_serial):
                    t.wait_ge(v_t1.h, vt1_v)
                    if vo_serial > 0:
                        t.wait_ge(v_out.h, vo_serial)
                    h2 = q // 4
                    qq = q % 4
                    for m in range(KB):
                        mm = t.transpose(pdt[:, 128 * m:128 * (m + 1)],
                                         t1[:, m * 2 + h2, 128 * qq:128 * (qq + 1)], eye_f[:])
                    t_tr.tag(mm)
                op("tensor", tr_t1)
                sout_g = st_out.count - 16 if jq >= 2 else -1
                vo_v = v_out.reserve()
                def mk_out(v, slot=slot, ttr_v=ttr_v, x2_v=x2_v, sout_g=sout_g):
                    v.wait_ge(t_tr.h, ttr_v)
                    v.wait_ge(x_ld.h, x2_v)
                    if sout_g >= 0:
                        v.wait_ge(st_out.h, sout_g + 16)
                    v_out.tag(v.tensor_add(outch[slot][:], xch2[slot][:], pdt[:]))
                op("vector", mk_out)
                st_out.reserve(16)
                def st_o(sy, b=b, q=q, slot=slot, vo_v=vo_v):
                    sy.wait_ge(v_out.h, vo_v)
                    st_out.tag(sy.dma_start(out=out_t[b, 128 * q:128 * (q + 1), :],
                                            in_=outch[slot][:]), 16)
                op("sync", st_o)
        FIN = st_out.count
        def fin_wait(sy):
            sy.wait_ge(st_out.h, FIN)
        op("sync", fin_wait)

    if DUMP:
        dmp1 = nc.dram_tensor("dump_ag1", [96 * NC, TK], BF16, kind="ExternalOutput").ap()
        dmp2 = nc.dram_tensor("dump_ag2", [D * NC, TK], BF16, kind="ExternalOutput").ap()
        dmp3 = nc.dram_tensor("dump_ag3", [DK * NC, L], BF16, kind="ExternalOutput").ap()
        dmp3b = nc.dram_tensor("dump_ag3b", [3, DK * NC, L], BF16, kind="ExternalOutput").ap()
        dmp1b = nc.dram_tensor("dump_ag1b", [3, 96 * NC, TK], BF16, kind="ExternalOutput").ap()
        dmp2b = nc.dram_tensor("dump_ag2b", [3, D * NC, TK], BF16, kind="ExternalOutput").ap()
        dmp2i = nc.dram_tensor("dump_ag2i", [B, D, TK], BF16, kind="ExternalOutput").ap()
        dmpxt = nc.dram_tensor("dump_xnts", [B, D, TK], BF16, kind="ExternalOutput").ap()
        dval = st_out.count
        st_out.reserve(320)
        def dump(g):
            g.wait_ge(st_out.h, dval)
            g.wait_ge(cc1.h, cc1.count)
            g.wait_ge(cc2.h, cc2.count)
            g.wait_ge(cc3.h, cc3.count)
            st_out.tag(g.dma_start(out=dmp1, in_=ag1o[0]), 16)
            st_out.tag(g.dma_start(out=dmp2, in_=ag2o[0]), 16)
            st_out.tag(g.dma_start(out=dmp3, in_=ag3o[0]), 16)
            for _bb in range(1, 4):
                st_out.tag(g.dma_start(out=dmp3b[_bb - 1], in_=ag3o[_bb]), 16)
                st_out.tag(g.dma_start(out=dmp1b[_bb - 1], in_=ag1o[_bb]), 16)
                st_out.tag(g.dma_start(out=dmp2b[_bb - 1], in_=ag2o[_bb]), 16)
            for _bb in range(B):
                st_out.tag(g.dma_start(out=dmp2i[_bb], in_=ag2i[_bb]), 16)
                st_out.tag(g.dma_start(out=dmpxt[_bb], in_=xnts[_bb]), 16)
        op("gpsimd", dump)
        DFIN = st_out.count
        def dump_wait(sy):
            sy.wait_ge(st_out.h, DFIN)
        op("sync", dump_wait)

    block = ctx.enter_context(nc.Block())

    @block.sync
    def _(sync):
        for eng, fn in ops:
            if eng == "sync":
                fn(sync)

    @block.gpsimd
    def _(gpsimd):
        for eng, fn in ops:
            if eng == "gpsimd":
                fn(gpsimd)

    @block.vector
    def _(vector):
        for eng, fn in ops:
            if eng == "vector":
                fn(vector)

    @block.scalar
    def _(scalar):
        for eng, fn in ops:
            if eng == "scalar":
                fn(scalar)

    @block.tensor
    def _(tensor):
        for eng, fn in ops:
            if eng == "tensor":
                fn(tensor)

    ctx.close()
    return nc


_NC_CACHE = {}


def kernel(L0_attn_x, L0_attn_W_xproj, L0_attn_W_dt, L0_attn_b_dt, L0_attn_A_log,
           L0_attn_D_param, L0_ffn_wf, L0_ffn_uf, L0_ffn_bf, L0_ffn_wi, L0_ffn_ui,
           L0_ffn_bi, L0_ffn_wc, L0_ffn_uc, L0_ffn_bc, L0_ffn_wo, L0_ffn_uo,
           L0_ffn_bo, L0_ffn_in_c):
    if "nc" not in _NC_CACHE:
        _NC_CACHE["nc"] = build_program()
    nc = _NC_CACHE["nc"]

    x = np.ascontiguousarray(np.asarray(L0_attn_x, dtype=np.float32))
    eye_f = np.eye(128, dtype=np.float32)
    eye_b = np.eye(128, dtype=ml_dtypes.bfloat16)
    sel = np.zeros((N, N, 128), dtype=ml_dtypes.bfloat16)
    for n in range(N):
        sel[n, n, :] = 1
    sel = np.ascontiguousarray(sel.reshape(N, N * 128))
    bias_pack = np.stack([np.asarray(v, dtype=np.float32).reshape(KB, 128).T
                          for v in (L0_ffn_bf, L0_ffn_bi, L0_ffn_bc, L0_ffn_bo, L0_ffn_in_c)],
                         axis=2)
    bias_pack = np.ascontiguousarray(bias_pack)
    wbig = [np.ascontiguousarray(np.asarray(w, dtype=np.float32).T) for w in
            (L0_ffn_wf, L0_ffn_uf, L0_ffn_wi, L0_ffn_ui, L0_ffn_wc, L0_ffn_uc,
             L0_ffn_wo, L0_ffn_uo)]
    wx = np.ascontiguousarray(np.asarray(L0_attn_W_xproj, dtype=np.float32).T)

    in_maps = []
    for c in range(NC):
        dsl = slice(DK * c, DK * (c + 1))
        m = {
            "x_in": np.ascontiguousarray(x[:, TK * c:TK * (c + 1), :]),
            "wx_in": wx,
            "wdt_in": np.ascontiguousarray(np.asarray(L0_attn_W_dt, dtype=np.float32)[dsl, :].T),
            "bdt_in": np.ascontiguousarray(np.asarray(L0_attn_b_dt, dtype=np.float32)[dsl].reshape(DK, 1)),
            "alog_in": np.ascontiguousarray(np.asarray(L0_attn_A_log, dtype=np.float32)[dsl, :]),
            "dvec_in": np.ascontiguousarray(np.asarray(L0_attn_D_param, dtype=np.float32)[dsl].reshape(DK, 1)),
            "bias_in": bias_pack,
            "eye_f_in": eye_f,
            "eye_b_in": eye_b,
            "sel_in": sel,
        }
        for g in range(8):
            m[f"w{g}_in"] = wbig[g]
        in_maps.append(m)

    res = run_bass_kernel_spmd(nc, in_maps, core_ids=list(range(NC)))
    return np.concatenate([np.asarray(r["out"]) for r in res.results], axis=1)


# version 13 (final)
# speedup vs baseline: 1.0158x; 1.0034x over previous
import numpy as np
import ml_dtypes
from contextlib import ExitStack

import concourse.bass as bass
import concourse.mybir as mybir
from concourse.bass_utils import run_bass_kernel_spmd

F32 = mybir.dt.float32
BF16 = mybir.dt.bfloat16
AF = mybir.ActivationFunctionType
ALU = mybir.AluOpType

NC = 8
B = 4
L = 8192
D = 1024
N = 16
R = 64
TK = L // NC
DK = D // NC
CH = 1024
NCH = L // CH
KB = D // 128


class Sem:

    def __init__(self, nc, name):
        self.h = nc.alloc_semaphore(name)
        self.count = 0

    def reserve(self, k=1):
        self.count += k
        return self.count

    def tag(self, instr, k=1):
        instr.then_inc(self.h, k)


STAGE = 3
DUMP = 0
SUB = 3
SUBY = 3


def build_program():
    nc = bass.Bass("TRN2", target_bir_lowering=False, debug=False, num_devices=NC)

    x_in = nc.dram_tensor("x_in", [B, TK, D], F32, kind="ExternalInput").ap()
    wx_in = nc.dram_tensor("wx_in", [D, 96], F32, kind="ExternalInput").ap()
    wdt_in = nc.dram_tensor("wdt_in", [R, DK], F32, kind="ExternalInput").ap()
    bdt_in = nc.dram_tensor("bdt_in", [DK, 1], F32, kind="ExternalInput").ap()
    alog_in = nc.dram_tensor("alog_in", [DK, N], F32, kind="ExternalInput").ap()
    dvec_in = nc.dram_tensor("dvec_in", [DK, 1], F32, kind="ExternalInput").ap()
    wbig_in = [nc.dram_tensor(f"w{g}_in", [D, D], F32, kind="ExternalInput").ap()
               for g in range(8)]
    bias_in = nc.dram_tensor("bias_in", [128, KB, 5], F32, kind="ExternalInput").ap()
    eye_f_in = nc.dram_tensor("eye_f_in", [128, 128], F32, kind="ExternalInput").ap()
    eye_b_in = nc.dram_tensor("eye_b_in", [128, 128], BF16, kind="ExternalInput").ap()
    sel_in = nc.dram_tensor("sel_in", [N, N * 128], BF16, kind="ExternalInput").ap()
    out_t = nc.dram_tensor("out", [B, TK, D], F32, kind="ExternalOutput").ap()

    ag1i = [nc.dram_tensor(f"ag1i_{b}", [96, TK], BF16).ap() for b in range(B)]
    ag1o = [nc.dram_tensor(f"ag1o_{b}", [96 * NC, TK], BF16).ap() for b in range(B)]
    ag2i = [nc.dram_tensor(f"ag2i_{b}", [D, TK], BF16).ap() for b in range(B)]
    ag2o = [nc.dram_tensor(f"ag2o_{b}", [D * NC, TK], BF16).ap() for b in range(B)]
    ag3i = [nc.dram_tensor(f"ag3i_{b}", [DK, L], BF16).ap() for b in range(B)]
    ag3o = [nc.dram_tensor(f"ag3o_{b}", [DK * NC, L], BF16).ap() for b in range(B)]
    xnts = nc.dram_tensor("xnts", [B, D, TK], BF16).ap()
    dump_dt = nc.dram_tensor("dump_dt", [B, 128, L], BF16, kind="ExternalOutput").ap() if DUMP else None
    dump_dtx = nc.dram_tensor("dump_dtx", [B, 128, L], BF16, kind="ExternalOutput").ap() if DUMP else None
    dump_br = nc.dram_tensor("dump_br", [128, CH], BF16, kind="ExternalOutput").ap() if DUMP else None
    dump_a = nc.dram_tensor("dump_a", [128, CH], F32, kind="ExternalOutput").ap() if DUMP else None
    dump_u = nc.dram_tensor("dump_u", [128, CH], BF16, kind="ExternalOutput").ap() if DUMP else None
    dump_h = nc.dram_tensor("dump_h", [128, CH], BF16, kind="ExternalOutput").ap() if DUMP else None
    dump_carry = nc.dram_tensor("dump_carry", [128, 16], F32, kind="ExternalOutput").ap() if DUMP else None
    dump_carry2 = nc.dram_tensor("dump_carry2", [128, 16], F32, kind="ExternalOutput").ap() if DUMP else None
    dump_cr = nc.dram_tensor("dump_cr", [128, CH], BF16, kind="ExternalOutput").ap() if DUMP else None
    dump_cc = nc.dram_tensor("dump_cc", [16, CH], BF16, kind="ExternalOutput").ap() if DUMP else None
    dump_xs = nc.dram_tensor("dump_xs", [128, CH], BF16, kind="ExternalOutput").ap() if DUMP else None
    if DUMP >= 5:
        dump_brall = nc.dram_tensor("dump_brall", [8, 16, 128, CH], BF16, kind="ExternalOutput").ap()
        dump_aall = nc.dram_tensor("dump_aall", [8, 16, 128, CH], F32, kind="ExternalOutput").ap()
        dump_uall = nc.dram_tensor("dump_uall", [8, 16, 128, CH], BF16, kind="ExternalOutput").ap()
        dump_hall = nc.dram_tensor("dump_hall", [8, 16, 128, CH], BF16, kind="ExternalOutput").ap()
        dump_crall = nc.dram_tensor("dump_crall", [8, 16, 128, CH], BF16, kind="ExternalOutput").ap()
        dump_pall = nc.dram_tensor("dump_pall", [8, 16, 128, CH], BF16, kind="ExternalOutput").ap()
        dump_yball = nc.dram_tensor("dump_yball", [8, 128, CH], BF16, kind="ExternalOutput").ap()
    wts = [nc.dram_tensor(f"wts_{g}", [D, D], BF16).ap() for g in range(8)]
    fence_d1 = nc.dram_tensor("fence_d1", [64, CH], BF16).ap()
    fence_d2 = nc.dram_tensor("fence_d2", [N, CH], BF16).ap()
    fence_d3 = nc.dram_tensor("fence_d3", [128, CH], BF16).ap()
    fence_d4 = nc.dram_tensor("fence_d4", [128, CH], BF16).ap()

    ctx = ExitStack()
    sb = lambda name, shape, dt: ctx.enter_context(nc.sbuf_tensor(name, shape, dt))

    wxT = sb("wxT", [128, KB, 96], BF16)
    wdtT = sb("wdtT", [64, 128], BF16)
    bdt_t = sb("bdt_t", [DK, 1], F32)
    alog_t = sb("alog_t", [DK, N], F32)
    a_mat = sb("a_mat", [DK, N], F32)
    dvec_t = sb("dvec_t", [DK, 1], F32)
    eye_f = sb("eye_f", [128, 128], F32)
    eye_b = sb("eye_b", [128, 128], BF16)
    sel_t = sb("sel_t", [N, N * 128], BF16)
    bias_t = sb("bias_t", [128, KB, 5], F32)

    dhi = sb("dhi", [128, 128], BF16)
    dlo = sb("dlo", [128, 128], BF16)

    wg_w = sb("wg_w", [128, KB, D], BF16)
    wg_u = sb("wg_u", [128, KB, D], BF16)

    xch = [sb(f"xch{i}", [128, D], F32) for i in range(2)]
    sqch = sb("sqch", [128, D], F32)
    xnf = sb("xnf", [128, D], F32)

    st_bn = sb("st_bn", [128, 12], F32)
    st_mv = sb("st_mv", [128, 2], F32)
    st_sum = sb("st_sum", [128, 1], F32)
    st_sq = sb("st_sq", [128, 1], F32)
    st_mean = sb("st_mean", [128, 1], F32)
    st_ex2 = sb("st_ex2", [128, 1], F32)
    st_m2 = sb("st_m2", [128, 1], F32)
    st_var = sb("st_var", [128, 1], F32)
    st_std = sb("st_std", [128, 1], F32)
    st_rstd = sb("st_rstd", [128, 1], F32)
    eps_t = sb("eps_t", [128, 1], F32)
    xnT = sb("xnT", [128, KB, TK], BF16)


    c_a = sb("c_a", [128, KB, TK], BF16)
    arena = sb("arena", [128, 8192], F32)
    xnbf = arena[:, 4096:4608].bitcast(BF16)
    xdbf = arena[0:96, 4608:5120].bitcast(BF16)
    ddiag = arena[:, 5120:5248]
    c_b = arena[:, 0:4096].bitcast(BF16).rearrange("p (k t) -> p k t", k=KB)
    gi_full = arena[:, 4096:8192].bitcast(BF16).rearrange("p (k t) -> p k t", k=KB)
    t1 = arena[:].rearrange("p (m t) -> p m t", m=16)
    gf_t = sb("gf_t", [128, 512], BF16)
    gg_t = sb("gg_t", [128, 512], BF16)
    cprod = sb("cprod", [128, 512], BF16)

    dtlo_r = [sb(f"dtlo{i}", [64, CH], BF16) for i in range(2)]
    ccomp_r = [sb(f"ccomp{i}", [N, CH], BF16) for i in range(2)]
    xnsl_r = [sb(f"xnsl{i}", [128, CH], BF16) for i in range(2)]
    dtbf_r = [sb(f"dtbf{i}", [128, CH], BF16) for i in range(2)]
    dtx_r = [sb(f"dtx{i}", [128, CH], BF16) for i in range(2)]
    et_t = sb("et_t", [128, 512], F32)
    dhi_f = et_t[:, 0:128]
    dlo_f = et_t[:, 128:256]
    a_ring = [sb(f"a_ring{i}", [128, CH], F32) for i in range(2)]
    brep = [sb(f"brep{i}", [128, CH], BF16) for i in range(5)]
    u_t = sb("u_t", [128, CH], BF16)
    h_t = sb("h_t", [128, CH], BF16)
    prod = [sb(f"prod{i}", [128, CH], BF16) for i in range(3)]
    crep = [sb(f"crep{i}", [128, CH], BF16) for i in range(2)]
    carry = sb("carry", [128, N], F32)
    ybf_r = [sb(f"ybf{i}", [128, CH], BF16) for i in range(2)]

    ybT = sb("ybT", [128, KB, TK], BF16)
    hch = [sb(f"hch{i}", [128, 512], F32) for i in range(2)]
    xch2 = xch
    outch = [xnf, sqch]

    pgate = ctx.enter_context(nc.psum_tensor("pgate", [128, 1024], F32))
    pdt = ctx.enter_context(nc.psum_tensor("pdt", [128, 1024], F32))
    pc = ctx.enter_context(nc.psum_tensor("pc", [128, 1024], F32))
    py = ctx.enter_context(nc.psum_tensor("py", [128, 1024], F32))

    Sm = lambda nm: Sem(nc, nm)
    x_ld = Sm("x_ld"); xt_ld = Sm("xt_ld"); misc_ld = Sm("misc_ld")
    w_ld = Sm("w_ld"); wt_tr = Sm("wt_tr"); wt_sp = Sm("wt_sp")
    v_xn = Sm("v_xn"); s_stat = Sm("s_stat"); v_stat = Sm("v_stat"); s_xnbf = Sm("s_xnbf")
    ag1w = Sm("ag1w"); ag2w = Sm("ag2w"); ag3w = Sm("ag3w"); xnts_w = Sm("xnts_w")
    cc1 = Sm("cc1"); cc2 = Sm("cc2"); cc3 = Sm("cc3")
    t_xd = Sm("t_xd"); s_xd = Sm("s_xd")
    g_ld = Sm("g_ld"); sc_ld = Sm("sc_ld")
    t_dt = Sm("t_dt"); s_dtexp = Sm("s_dtexp"); s_dt = Sm("s_dt"); v_dtx = Sm("v_dtx")
    g_brep = Sm("g_brep"); s_a = Sm("s_a"); v_u = Sm("v_u"); v_h = Sm("v_h")
    t_cr = Sm("t_cr"); s_cre = Sm("s_cre"); v_p = Sm("v_p"); t_p = Sm("t_p")
    t_yf = Sm("t_yf"); v_ye = Sm("v_ye"); t_D = Sm("t_D")
    t_gate = Sm("t_gate"); s_gate = Sm("s_gate"); v_c = Sm("v_c")
    wg_ld = Sm("wg_ld")
    t_wo = Sm("t_wo"); s_h = Sm("s_h"); v_t1 = Sm("v_t1"); g_yb = Sm("g_yb")
    t_tr = Sm("t_tr"); v_out = Sm("v_out"); st_out = Sm("st_out")
    s_prep = Sm("s_prep"); v_prep = Sm("v_prep"); v_drain = Sm("v_drain"); v_cz = Sm("v_cz"); d_dt = Sm("d_dt")
    v_ci = Sm("v_ci")
    t_xnt = Sm("t_xnt"); v_xnt = Sm("v_xnt")

    ops = []
    def op(eng, fn):
        ops.append((eng, fn))

    _dyn_cache = {}
    def dynslice(eng, size):
        key = size
        if key not in _dyn_cache:
            _dyn_cache[key] = bass.ts(eng.partition_id(), size)
        return _dyn_cache[key]

    RG = [list(range(NC))]
    gate_act_vals = []

    def ld_small(sc):
        for src, dst in ((bdt_in, bdt_t), (alog_in, alog_t), (dvec_in, dvec_t),
                         (eye_f_in, eye_f), (eye_b_in, eye_b), (sel_in, sel_t),
                         (bias_in, bias_t)):
            d = sc.dma_start(out=dst[:], in_=src)
            misc_ld.tag(d, 16)
    for _ in range(7):
        misc_ld.reserve(16)
    MV = misc_ld.count
    op("scalar", ld_small)

    def ld_cast0(g):
        w_ld.tag(g.dma_start(out=wxT[:], in_=wx_in.rearrange("(k p) c -> p k c", p=128)), 16)
        w_ld.tag(g.dma_start(out=wdtT[:], in_=wdt_in), 16)
    w_ld.reserve(32)
    W2 = w_ld.count
    op("gpsimd", ld_cast0)

    def prep_v(v):
        v_prep.tag(v.memset(eps_t[:], 1e-5))
    VP_MEMSET = v_prep.reserve()
    op("vector", prep_v)

    def prep_s(s):
        s.wait_ge(misc_ld.h, MV)
        s_prep.tag(s.activation(a_mat[:], alog_t[:], AF.Exp))
    SP0 = s_prep.reserve()
    op("scalar", prep_s)

    def prep_v2(v):
        v.wait_ge(s_prep.h, SP0)
        v.tensor_scalar(a_mat[:], a_mat[:], -1.0, None, ALU.mult)
        v_prep.tag(v.tensor_scalar(ddiag[:], eye_f[:], dvec_t[:], None, ALU.mult))
    VP0 = v_prep.reserve()
    op("vector", prep_v2)

    def prep_s2(s):
        s.wait_ge(v_prep.h, VP0)
        s.copy(dhi[:], ddiag[:])
        s_prep.tag(s.copy(dhi_f[:], dhi[:]))
    SP1 = s_prep.reserve()
    op("scalar", prep_s2)

    def prep_v3(v):
        v.wait_ge(s_prep.h, SP1)
        v_prep.tag(v.tensor_sub(dlo_f[:], ddiag[:], dhi_f[:]))
    VP1 = v_prep.reserve()
    op("vector", prep_v3)

    def prep_s3(s):
        s.wait_ge(v_prep.h, VP1)
        s_prep.tag(s.copy(dlo[:], dlo_f[:]))
    SPF = s_prep.reserve()
    op("scalar", prep_s3)

    WT_BASE = 0

    def weight_triplet(g):
        sp_guard = wt_sp.count
        def ld_w(gp, g=g, sp_guard=sp_guard, nb=g // 2 + 1):
            gp.wait_ge(cc2.h, nb)
            gp.wait_ge(cc1.h, nb)
            if sp_guard > 0:
                gp.wait_ge(wt_sp.h, sp_guard)
            w_ld.tag(gp.dma_start(out=wg_u[:], in_=wbig_in[g].rearrange("(k p) c -> p k c", p=128)), 16)
        wld_v = w_ld.reserve(16)
        op("gpsimd", ld_w)
        def sp_w(sc, g=g, wld_v=wld_v):
            sc.wait_ge(w_ld.h, wld_v)
            wt_sp.tag(sc.dma_start(out=wts[g].rearrange("(i p) c -> p i c", p=128), in_=wg_u[:]), 16)
        wt_sp.reserve(16)
        op("scalar", sp_w)

    for b in range(B):
        for q in range(KB):
            jq = b * KB + q
            slot = q % 2
            xc = xch[slot]
            vg = v_xn.count - 1 if jq >= 2 else -1
            xval = x_ld.reserve(16)
            def ldx(sy, b=b, q=q, xc=xc, vg=vg):
                if vg >= 0:
                    sy.wait_ge(v_xn.h, vg)
                x_ld.tag(sy.dma_start(out=xc[:], in_=x_in[b, 128 * q:128 * (q + 1), :]), 16)
            op("sync", ldx)
            sstd_guard = s_stat.count
            d1 = v_drain.reserve(); d2 = v_drain.reserve(); d3 = v_drain.reserve()
            vstat_v = v_stat.reserve()
            def stats_v(v, xc=xc, xval=xval, d1=d1, d2=d2, d3=d3,
                        sstd_guard=sstd_guard, jq_=jq):
                v.wait_ge(x_ld.h, xval)
                if sstd_guard > 0:
                    v.wait_ge(s_stat.h, sstd_guard)
                v_drain.tag(v.bn_stats(st_bn[:, 0:6], xc[:, 0:512]))
                v.wait_ge(v_drain.h, d1)
                v_drain.tag(v.bn_stats(st_bn[:, 6:12], xc[:, 512:1024]))
                v.wait_ge(v_drain.h, d2)
                v_drain.tag(v.bn_aggr(st_mv[:], st_bn[:]))
                v.wait_ge(v_drain.h, d3)
                v_stat.tag(v.tensor_copy(st_mean[:], st_mv[:, 0:1]))
            op("vector", stats_v)
            vxn_guard = v_xn.count
            sstd_v = s_stat.reserve()
            def std_s(s, vstat_v=vstat_v, vxn_guard=vxn_guard):
                s.wait_ge(v_stat.h, vstat_v)
                if vxn_guard > 0:
                    s.wait_ge(v_xn.h, vxn_guard)
                s.wait_ge(v_prep.h, 1)
                s.activation(st_std[:], st_mv[:, 1:2], AF.Ln, bias=eps_t[:])
                s_stat.tag(s.activation(st_rstd[:], st_std[:], AF.Exp, scale=-0.5))
            op("scalar", std_s)
            sbf_guard = s_xnbf.count
            vxn_v = v_xn.reserve()
            def norm_v(v, xc=xc, sstd_v=sstd_v, sbf_guard=sbf_guard):
                v.wait_ge(s_stat.h, sstd_v)
                if sbf_guard > 0:
                    v.wait_ge(s_xnbf.h, sbf_guard)
                v_xn.tag(v.tensor_scalar(xnf[:], xc[:], st_mean[:], st_rstd[:],
                                         ALU.subtract, ALU.mult))
            op("vector", norm_v)
            xt_guard = t_xnt.count
            sbf_v = s_xnbf.reserve()
            def tobf_s(s, vxn_v=vxn_v, xt_guard=xt_guard):
                s.wait_ge(v_xn.h, vxn_v)
                if xt_guard > 0:
                    s.wait_ge(t_xnt.h, xt_guard)
                s_xnbf.tag(s.copy(xnbf[:], xnf[:]))
            op("scalar", tobf_s)
            txd_guard = t_xd.count
            vxnt_guard = v_xnt.count
            txnt_v = t_xnt.reserve()
            def tr_xn(t, q=q, sbf_v=sbf_v, txd_guard=txd_guard, vxnt_guard=vxnt_guard):
                t.wait_ge(s_xnbf.h, sbf_v)
                if vxnt_guard > 0:
                    t.wait_ge(v_xnt.h, vxnt_guard)
                pcb = pc[:].bitcast(BF16)
                for k in range(KB):
                    mm = t.transpose(pcb[:, 128 * k:128 * (k + 1)],
                                     xnbf[:, 128 * k:128 * (k + 1)], eye_b[:])
                t_xnt.tag(mm)
            op("tensor", tr_xn)
            vxnt_v = v_xnt.reserve()
            def cp_xn(v, b=b, q=q, txnt_v=txnt_v, txd_guard=txd_guard):
                v.wait_ge(t_xnt.h, txnt_v)
                if q == 0 and txd_guard > 0:
                    v.wait_ge(t_xd.h, txd_guard)
                if q == 0 and b > 0:
                    v.wait_ge(ag2w.h, 16 * b)
                    v.wait_ge(xnts_w.h, 16 * b)
                v_xnt.tag(v.tensor_copy(xnT[:, :, 128 * q:128 * (q + 1)],
                                        pc[:].bitcast(BF16)[:, 0:1024].rearrange("p (k c) -> p k c", k=KB)))
            op("vector", cp_xn)
        xt_v = v_xnt.count
        ag2_v = ag2w.reserve(16)
        xnts_w.reserve(16)
        def wr_xn(sy, b=b, xt_v=xt_v):
            sy.wait_ge(v_xnt.h, xt_v)
            ag2w.tag(sy.dma_start(out=ag2i[b].rearrange("(k p) t -> p k t", p=128), in_=xnT[:]), 16)
            xnts_w.tag(sy.dma_start(out=xnts[b].rearrange("(k p) t -> p k t", p=128), in_=xnT[:]), 16)
        op("scalar", wr_xn)
        cc2.reserve(1)
        def coll2(g, b=b, ag2_v=ag2_v, wld_all=w_ld.count, sp_all=wt_sp.count):
            g.wait_ge(ag2w.h, ag2_v)
            g.wait_ge(w_ld.h, wld_all)
            if sp_all > 0:
                g.wait_ge(wt_sp.h, sp_all)
            if b > 0:
                g.wait_ge(cc1.h, b)
            cc2.tag(g.collective_compute("AllGather", ALU.bypass, replica_groups=RG,
                                         ins=[ag2i[b]], outs=[ag2o[b]]))
        op("gpsimd", coll2)
        for h2 in range(2):
            sxd_g = s_xd.count
            txd_v = t_xd.reserve()
            first_t = (b == 0 and h2 == 0)
            def mm_xd(t, h2=h2, xt_v=xt_v, sxd_g=sxd_g, first_t=first_t):
                if first_t:
                    t.wait_ge(misc_ld.h, MV)
                    t.wait_ge(s_prep.h, SPF)
                t.wait_ge(v_xnt.h, xt_v)
                t.wait_ge(w_ld.h, W2)
                if sxd_g > 0:
                    t.wait_ge(s_xd.h, sxd_g)
                sl = slice(512 * h2, 512 * (h2 + 1))
                for k in range(KB):
                    mm = t.matmul(pdt[0:96, sl], wxT[:, k, :], xnT[:, k, sl],
                                  start=(k == 0), stop=(k == KB - 1))
                t_xd.tag(mm)
            op("tensor", mm_xd)
            s_xd.reserve()
            def ev_xd(s, b=b, h2=h2, txd_v=txd_v):
                s.wait_ge(t_xd.h, txd_v)
                if h2 == 0 and b > 0:
                    s.wait_ge(ag1w.h, 16 * b)
                sl = slice(512 * h2, 512 * (h2 + 1))
                s_xd.tag(s.copy(xdbf[:, sl], pdt[0:96, sl]))
            op("scalar", ev_xd)
        sxd_v = s_xd.count
        ag1_v = ag1w.reserve(16)
        def wr_xd(g, b=b, sxd_v=sxd_v):
            g.wait_ge(s_xd.h, sxd_v)
            ag1w.tag(g.dma_start(out=ag1i[b], in_=xdbf[:]), 16)
        op("scalar", wr_xd)
        cc1.reserve(1)
        def coll1(g, b=b, ag1_v=ag1_v, wld_all=w_ld.count, sp_all=wt_sp.count):
            g.wait_ge(ag1w.h, ag1_v)
            g.wait_ge(w_ld.h, wld_all)
            if sp_all > 0:
                g.wait_ge(wt_sp.h, sp_all)
            g.wait_ge(cc2.h, b + 1)
            cc1.tag(g.collective_compute("AllGather", ALU.bypass, replica_groups=RG,
                                         ins=[ag1i[b]], outs=[ag1o[b]]))
        op("gpsimd", coll1)
        weight_triplet(2 * b)
        weight_triplet(2 * b + 1)


    X_DBL_DONE = t_xd.count
    P1_AG2 = ag2w.count
    P1_XNTS = xnts_w.count
    P1_XNT = v_xnt.count
    ALL_SPILLS = wt_sp.count

    def make_ffn_step(b):
        c_old = c_a if b % 2 == 0 else c_b
        c_new = c_b if b % 2 == 0 else c_a
        units = []
        tgate_guard = t_gate.count
        xn_v = sc_ld.reserve(16)
        def ld_xn(sc, b=b, tgate_guard=tgate_guard):
            if b == 0:
                sc.wait_ge(t_xd.h, X_DBL_DONE)
                sc.wait_ge(ag2w.h, P1_AG2)
                sc.wait_ge(xnts_w.h, P1_XNTS)
            else:
                sc.wait_ge(t_gate.h, tgate_guard)
            sc.wait_ge(xnts_w.h, 16 * (b + 1))
            sc_ld.tag(sc.dma_start(out=xnT[:], in_=xnts[b].rearrange("(k p) t -> p k t", p=128)), 16)
        units.append(("scalar", ld_xn))
        cinit_v = -1
        if b == 0:
            cinit_v = v_c.reserve()
            vci_v = v_ci.reserve()
            def init_c(v, vci_v=vci_v):
                v.wait_ge(misc_ld.h, MV)
                v_ci.tag(v.memset(c_a[:], 0.0))
                v.wait_ge(v_ci.h, vci_v)
                for k in range(KB - 1):
                    v.tensor_scalar(c_a[:, k, :], c_a[:, k, :], bias_t[:, k, 4:5],
                                    None, ALU.add)
                k = KB - 1
                v_c.tag(v.tensor_scalar(c_a[:, k, :], c_a[:, k, :], bias_t[:, k, 4:5],
                                        None, ALU.add))
            units.append(("vector", init_c))
        for gi in range(3):
            func_is_sig = gi < 2
            bj = gi
            wld_guard = t_gate.count
            wgl_v = wg_ld.reserve(32)
            def ld_wg(sc, gi=gi, wld_guard=wld_guard):
                if wld_guard > 0:
                    sc.wait_ge(t_gate.h, wld_guard)
                sc.wait_ge(wt_sp.h, ALL_SPILLS)
                wg_ld.tag(sc.dma_start(out=wg_w[:], in_=wts[2 * gi].rearrange("(k p) c -> p k c", p=128)), 16)
                wg_ld.tag(sc.dma_start(out=wg_u[:], in_=wts[2 * gi + 1].rearrange("(k p) c -> p k c", p=128)), 16)
            units.append(("scalar", ld_wg))
            for m in range(KB):
                for h2 in range(2):
                    uglobal = len(gate_act_vals)
                    half_guard = gate_act_vals[uglobal - 2] if uglobal >= 2 else -1
                    vc_guard = v_c.count
                    tg_v = t_gate.reserve()
                    def mm_g(t, gi=gi, m=m, h2=h2, b=b, wgl_v=wgl_v, xn_v=xn_v,
                             half_guard=half_guard, c_old=c_old, vc_guard=vc_guard,
                             cinit_v=cinit_v):
                        t.wait_ge(wg_ld.h, wgl_v)
                        t.wait_ge(sc_ld.h, xn_v)
                        if half_guard >= 0:
                            t.wait_ge(s_gate.h, half_guard)
                        if b == 0:
                            t.wait_ge(v_c.h, cinit_v)
                        elif vc_guard > 0:
                            t.wait_ge(v_c.h, vc_guard)
                        sl = slice(512 * h2, 512 * (h2 + 1))
                        for k in range(KB):
                            t.matmul(pgate[:, sl], wg_w[:, k, 128 * m:128 * (m + 1)],
                                     xnT[:, k, sl], start=(k == 0), stop=False)
                        for k in range(KB):
                            mm = t.matmul(pgate[:, sl], wg_u[:, k, 128 * m:128 * (m + 1)],
                                          c_old[:, k, sl], start=False, stop=(k == KB - 1))
                        t_gate.tag(mm)
                    units.append(("tensor", mm_g))
                    vc_g2 = v_c.count
                    sg_v = s_gate.reserve()
                    gate_act_vals.append(sg_v)
                    def act_g(s, gi=gi, m=m, h2=h2, bj=bj, tg_v=tg_v, vc_g2=vc_g2,
                              func_is_sig=func_is_sig):
                        s.wait_ge(t_gate.h, tg_v)
                        if gi in (0, 2) and vc_g2 > 0:
                            s.wait_ge(v_c.h, vc_g2)
                        sl = slice(512 * h2, 512 * (h2 + 1))
                        if gi == 0:
                            dst = gf_t[:]
                        elif gi == 1:
                            dst = gi_full[:, m, sl]
                        else:
                            dst = gg_t[:]
                        s_gate.tag(s.activation(dst, pgate[:, sl],
                                                AF.Sigmoid if func_is_sig else AF.Tanh,
                                                bias=bias_t[:, m, bj:bj + 1]))
                    units.append(("scalar", act_g))
                    if gi == 0:
                        v_c.reserve()
                        def upd_f(v, m=m, h2=h2, sg_v=sg_v, c_old=c_old, c_new=c_new):
                            v.wait_ge(s_gate.h, sg_v)
                            sl = slice(512 * h2, 512 * (h2 + 1))
                            v_c.tag(v.tensor_mul(c_new[:, m, sl], gf_t[:], c_old[:, m, sl]))
                        units.append(("vector", upd_f))
                    elif gi == 2:
                        v_c.reserve()
                        def upd_g(v, m=m, h2=h2, sg_v=sg_v, c_new=c_new):
                            v.wait_ge(s_gate.h, sg_v)
                            sl = slice(512 * h2, 512 * (h2 + 1))
                            v.tensor_mul(cprod[:], gi_full[:, m, sl], gg_t[:])
                            v_c.tag(v.tensor_add(c_new[:, m, sl], c_new[:, m, sl], cprod[:]))
                        units.append(("vector", upd_g))
        return units

    for b in (range(B) if STAGE >= 2 else []):
        cc1_v = b + 1
        cc2_v = b + 1
        ffn_units = make_ffn_step(b)
        fu_iter = iter(ffn_units)
        fu_per_chunk = ((len(ffn_units) + NCH - 1) // NCH) if SUB >= 3 else 0

        vcz_v = v_cz.reserve()
        def zero_carry(v, b=b, vcz_v=vcz_v):
            v_cz.tag(v.memset(carry[:], 0.0))
            v.wait_ge(v_cz.h, vcz_v)
        op("vector", zero_carry)
        for c8 in range(NCH):
            it_base = b * NCH + c8
            rslot = c8 % 2
            tdt_guard = t_dt.count - 2 if it_base >= 2 else -1
            tcr_guard = t_cr.count - 16 if it_base >= 2 else -1
            vdtx_guard = v_dtx.count - 1 if it_base >= 2 else -1
            tD_guard = t_D.count - 1 if it_base >= 2 else -1
            gld_v = g_ld.reserve(96)
            ddt_g = d_dt.count
            def ld_chunk(g, b=b, c8=c8, rslot=rslot, cc1_v=cc1_v,
                         tdt_guard=tdt_guard, tcr_guard=tcr_guard, ddt_g=ddt_g):
                g.wait_ge(cc1.h, cc1_v)
                if tdt_guard >= 0:
                    g.wait_ge(t_dt.h, tdt_guard)
                if tcr_guard >= 0:
                    g.wait_ge(t_cr.h, tcr_guard)
                if DUMP and ddt_g > 0:
                    g.wait_ge(d_dt.h, ddt_g)
                g_ld.tag(g.dma_start(out=dtlo_r[rslot][:],
                                     in_=ag1o[b][96 * c8:96 * c8 + 64, :]), 16)
                g_ld.tag(g.dma_start(out=ccomp_r[rslot][:],
                                     in_=ag1o[b][96 * c8 + 80:96 * (c8 + 1), :]), 16)
            op("sync", ld_chunk)
            def ld_xnsl(sy, b=b, c8=c8, rslot=rslot, cc2_v=cc2_v,
                        vdtx_guard=vdtx_guard, tD_guard=tD_guard):
                sy.wait_ge(cc2.h, cc2_v)
                if vdtx_guard >= 0:
                    sy.wait_ge(v_dtx.h, vdtx_guard)
                if tD_guard >= 0:
                    sy.wait_ge(t_D.h, tD_guard)
                src2 = ag2o[b].rearrange("(r z) t -> z r t", z=D)[:, c8, :]
                g_ld.tag(sy.dma_start(out=xnsl_r[rslot][:],
                                      in_=src2[dynslice(sy, 128), :]), 16)
                g_ld.tag(sy.dma_start(out=fence_d1, in_=dtlo_r[rslot][:]), 16)
                g_ld.tag(sy.dma_start(out=fence_d2, in_=ccomp_r[rslot][:]), 16)
                g_ld.tag(sy.dma_start(out=fence_d3, in_=xnsl_r[rslot][:]), 16)
            op("sync", ld_xnsl)
            for half in range(2):
                uu = it_base * 2 + half
                sdte_g = s_dtexp.count - 1 if uu >= 2 else -1
                tdt_v = t_dt.reserve()
                def mm_dt(t, rslot=rslot, half=half, gld_v=gld_v, sdte_g=sdte_g):
                    t.wait_ge(g_ld.h, gld_v)
                    t.wait_ge(s_xd.h, X_DBL_DONE)
                    if sdte_g >= 0:
                        t.wait_ge(s_dtexp.h, sdte_g)
                    sl = slice(512 * half, 512 * (half + 1))
                    t_dt.tag(t.matmul(pdt[:, sl], wdtT[0:64, :], dtlo_r[rslot][:, sl],
                                      start=True, stop=True))
                op("tensor", mm_dt)
                vdtx_g2 = v_dtx.count - 1 if it_base >= 2 else -1
                s_dtexp.reserve()
                sdt_half = s_dt.reserve()
                def sp_dt(s, rslot=rslot, half=half, tdt_v=tdt_v, vdtx_g2=vdtx_g2):
                    s.wait_ge(t_dt.h, tdt_v)
                    s.wait_ge(misc_ld.h, MV)
                    if half == 0 and vdtx_g2 >= 0:
                        s.wait_ge(v_dtx.h, vdtx_g2)
                    sl = slice(512 * half, 512 * (half + 1))
                    s_dtexp.tag(s.activation(et_t[:], pdt[:, sl], AF.Exp, bias=bdt_t[:]))
                    s_dt.tag(s.activation(dtbf_r[rslot][:, sl], et_t[:], AF.Ln, bias=1.0))
                op("scalar", sp_dt)
            sdt_v = s_dt.count
            if DUMP:
                d_dt.reserve(16)
                def dmp_dt(sy, b=b, c8=c8, rslot=rslot, sdt_v=sdt_v):
                    sy.wait_ge(s_dt.h, sdt_v)
                    d_dt.tag(sy.dma_start(out=dump_dt[b][:, CH * c8:CH * (c8 + 1)],
                                          in_=dtbf_r[rslot][:]), 16)
                op("sync", dmp_dt)
            vdtx_v = v_dtx.reserve()
            def mk_dtx(v, rslot=rslot, sdt_v=sdt_v, gld_v=gld_v):
                v.wait_ge(s_dt.h, sdt_v)
                v.wait_ge(g_ld.h, gld_v)
                v_dtx.tag(v.tensor_mul(dtx_r[rslot][:], dtbf_r[rslot][:], xnsl_r[rslot][:]))
            op("vector", mk_dtx)
            if DUMP:
                d_dt.reserve(16)
                def dmp_dtx(sy, b=b, c8=c8, rslot=rslot, vdtx_v=vdtx_v):
                    sy.wait_ge(v_dtx.h, vdtx_v)
                    d_dt.tag(sy.dma_start(out=dump_dtx[b][:, CH * c8:CH * (c8 + 1)],
                                          in_=dtx_r[rslot][:]), 16)
                op("sync", dmp_dtx)
            vye_g = v_ye.count
            t_D.reserve()
            def mm_D(t, rslot=rslot, vye_g=vye_g, gld_v=gld_v):
                if vye_g > 0:
                    t.wait_ge(v_ye.h, vye_g)
                t.wait_ge(g_ld.h, gld_v)
                for half in range(2):
                    sl = slice(512 * half, 512 * (half + 1))
                    t.matmul(py[:, sl], dhi[:], xnsl_r[rslot][:, sl], start=True, stop=False)
                    mm = t.matmul(py[:, sl], dlo[:], xnsl_r[rslot][:, sl], start=False, stop=False)
                t_D.tag(mm)
            op("tensor", mm_D)
            dgate = 0
            for n in (range(N) if SUB >= 1 else []):
                it = it_base * N + n
                bslot = it % 5
                vu_g = v_u.count - 4 if it >= 5 else -1
                gb_v = g_brep.reserve(16)
                def bro(g, b=b, c8=c8, n=n, bslot=bslot, vu_g=vu_g, cc1_v=cc1_v, dg=dgate):
                    g.wait_ge(cc1.h, cc1_v)
                    if dg > 0:
                        g.wait_ge(d_dt.h, dg)
                    if vu_g >= 0:
                        g.wait_ge(v_u.h, vu_g)
                    row = 96 * c8 + 64 + n
                    g_brep.tag(g.dma_start(out=brep[bslot][:],
                                           in_=ag1o[b][row:row + 1, :].partition_broadcast(128)), 16)
                op("sync", bro)
                if n == N - 1:
                    g_brep.reserve(16)
                    def bro_fence(sy, bslot=bslot):
                        g_brep.tag(sy.dma_start(out=fence_d4, in_=brep[bslot][:]), 16)
                    op("sync", bro_fence)
                aslot = it % 2
                vh_g = v_h.count - 1 if it >= 2 else -1
                sa_v = s_a.reserve()
                def mk_a(s, n=n, rslot=rslot, aslot=aslot, vh_g=vh_g, sdt_v=sdt_v, dg=dgate):
                    s.wait_ge(s_dt.h, sdt_v)
                    if dg > 0:
                        s.wait_ge(d_dt.h, dg)
                    if vh_g >= 0:
                        s.wait_ge(v_h.h, vh_g)
                    s_a.tag(s.activation(a_ring[aslot][:], dtbf_r[rslot][:], AF.Exp,
                                         scale=a_mat[:, n:n + 1]))
                op("scalar", mk_a)
                v_u.reserve()
                vh_v = v_h.reserve()
                def scan_v(v, n=n, rslot=rslot, bslot=bslot, aslot=aslot, gb_v=gb_v,
                           sa_v=sa_v, vdtx_v=vdtx_v, vh_v=vh_v, vcz_v=vcz_v, first=(c8 == 0),
                           dg=dgate):
                    if dg > 0:
                        v.wait_ge(d_dt.h, dg)
                    if first:
                        v.wait_ge(v_cz.h, vcz_v)
                    v.wait_ge(v_dtx.h, vdtx_v)
                    v.wait_ge(g_brep.h, gb_v + (0 if DUMP else 16))
                    v_u.tag(v.tensor_mul(u_t[:], dtx_r[rslot][:], brep[bslot][:]))
                    v.wait_ge(s_a.h, sa_v)
                    v_h.tag(v.tensor_tensor_scan(h_t[:], a_ring[aslot][:], u_t[:],
                                                 carry[:, n:n + 1], ALU.mult, ALU.add))
                    v.wait_ge(v_h.h, vh_v)
                    v.tensor_copy(carry[:, n:n + 1], h_t[:, CH - 1:CH])
                op("vector", scan_v)
                scre_g = s_cre.count if it >= 1 else -1
                tcr_v = t_cr.reserve()
                first_cr = (it == 0 and b == 0)
                def mk_cr(t, n=n, rslot=rslot, gld_v=gld_v, scre_g=scre_g, first_cr=first_cr):
                    t.wait_ge(g_ld.h, gld_v)
                    if first_cr:
                        t.wait_ge(v_xnt.h, P1_XNT)
                    if scre_g >= 0:
                        t.wait_ge(s_cre.h, scre_g)
                    for half in range(2):
                        sl = slice(512 * half, 512 * (half + 1))
                        mm = t.matmul(pc[:, sl], sel_t[:, 128 * n:128 * (n + 1)],
                                      ccomp_r[rslot][:, sl], start=True, stop=True)
                    t_cr.tag(mm)
                op("tensor", mk_cr)
                cslot = it % 2
                vp_g = v_p.count - 1 if it >= 2 else -1
                scre_v = s_cre.reserve()
                def ev_cr(s, cslot=cslot, tcr_v=tcr_v, vp_g=vp_g, dg=dgate):
                    s.wait_ge(t_cr.h, tcr_v)
                    if dg > 0:
                        s.wait_ge(d_dt.h, dg)
                    if vp_g >= 0:
                        s.wait_ge(v_p.h, vp_g)
                    s_cre.tag(s.copy(crep[cslot][:], pc[:]))
                op("scalar", ev_cr)
                pslot = it % 3
                tp_g = t_p.count - 2 if it >= 3 else -1
                vp_v = v_p.reserve()
                def mk_p(v, pslot=pslot, cslot=cslot, scre_v=scre_v, tp_g=tp_g, dg=dgate):
                    v.wait_ge(s_cre.h, scre_v)
                    if dg > 0:
                        v.wait_ge(d_dt.h, dg)
                    if tp_g >= 0:
                        v.wait_ge(t_p.h, tp_g)
                    v_p.tag(v.tensor_mul(prod[pslot][:], h_t[:], crep[cslot][:]))
                op("vector", mk_p)
                if DUMP and b == 0 and c8 == 0 and n == 0:
                    d_dt.reserve(16 * 8)
                    def dmp_sc(sy, bslot=bslot, aslot=aslot, cslot=cslot, rslot=rslot,
                               vh_v=vh_v, scre_v=scre_v, vp_v=vp_v):
                        sy.wait_ge(v_h.h, vh_v)
                        sy.wait_ge(s_cre.h, scre_v)
                        sy.wait_ge(v_p.h, vp_v)
                        d_dt.tag(sy.dma_start(out=dump_br, in_=brep[bslot][:]), 16)
                        d_dt.tag(sy.dma_start(out=dump_a, in_=a_ring[aslot][:]), 16)
                        d_dt.tag(sy.dma_start(out=dump_u, in_=u_t[:]), 16)
                        d_dt.tag(sy.dma_start(out=dump_h, in_=h_t[:]), 16)
                        d_dt.tag(sy.dma_start(out=dump_carry, in_=carry[:]), 16)
                        d_dt.tag(sy.dma_start(out=dump_cr, in_=crep[cslot][:]), 16)
                        d_dt.tag(sy.dma_start(out=dump_cc, in_=ccomp_r[rslot][:]), 16)
                        d_dt.tag(sy.dma_start(out=dump_xs, in_=xnsl_r[rslot][:]), 16)
                    op("sync", dmp_sc)
                    dgate = d_dt.count
                if DUMP >= 5 and b == 0:
                    d_dt.reserve(16 * 6)
                    def dmp_all(sy, c8=c8, n=n, bslot=bslot, aslot=aslot, cslot=cslot,
                                pslot=pslot, vh_v=vh_v, scre_v=scre_v, vp_v=vp_v):
                        sy.wait_ge(v_h.h, vh_v)
                        sy.wait_ge(s_cre.h, scre_v)
                        sy.wait_ge(v_p.h, vp_v)
                        d_dt.tag(sy.dma_start(out=dump_brall[c8, n], in_=brep[bslot][:]), 16)
                        d_dt.tag(sy.dma_start(out=dump_aall[c8, n], in_=a_ring[aslot][:]), 16)
                        d_dt.tag(sy.dma_start(out=dump_uall[c8, n], in_=u_t[:]), 16)
                        d_dt.tag(sy.dma_start(out=dump_hall[c8, n], in_=h_t[:]), 16)
                        d_dt.tag(sy.dma_start(out=dump_crall[c8, n], in_=crep[cslot][:]), 16)
                        d_dt.tag(sy.dma_start(out=dump_pall[c8, n], in_=prod[pslot][:]), 16)
                    op("sync", dmp_all)
                    dgate = d_dt.count
                if DUMP and b == 0 and c8 == 0 and n == N - 1:
                    d_dt.reserve(16)
                    def dmp_cy2(sy, vp_v=vp_v):
                        sy.wait_ge(v_p.h, vp_v)
                        d_dt.tag(sy.dma_start(out=dump_carry2, in_=carry[:]), 16)
                    op("sync", dmp_cy2)
                    dgate = d_dt.count
                if SUB < 2 or SUBY < 1:
                    continue
                t_p.reserve()
                def mk_y(t, n=n, pslot=pslot, vp_v=vp_v):
                    t.wait_ge(v_p.h, vp_v)
                    for half in range(2):
                        sl = slice(512 * half, 512 * (half + 1))
                        mm = t.matmul(py[:, sl], eye_b[:], prod[pslot][:, sl],
                                      start=False, stop=(n == N - 1))
                    t_p.tag(mm)
                op("tensor", mk_y)
            if SUB < 2 or SUBY < 2:
                continue
            tyf_v = t_p.count
            yslot = c8 % 2
            ag3_guard = ag3w.count - 16 if it_base >= 2 else -1
            vye_v = v_ye.reserve()
            def ev_y(v, yslot=yslot, tyf_v=tyf_v, ag3_guard=ag3_guard, dg=dgate):
                v.wait_ge(t_p.h, tyf_v)
                if dg > 0:
                    v.wait_ge(d_dt.h, dg)
                if ag3_guard >= 0:
                    v.wait_ge(ag3w.h, ag3_guard)
                v_ye.tag(v.tensor_copy(ybf_r[yslot][:], py[:]))
            op("vector", ev_y)
            if DUMP >= 5 and b == 0:
                d_dt.reserve(16)
                def dmp_ybf(sy, c8=c8, yslot=yslot, vye_v=vye_v):
                    sy.wait_ge(v_ye.h, vye_v)
                    d_dt.tag(sy.dma_start(out=dump_yball[c8], in_=ybf_r[yslot][:]), 16)
                op("sync", dmp_ybf)
                dgate = d_dt.count
            ag3w.reserve(16)
            def wr_y(sy, b=b, c8=c8, yslot=yslot, vye_v=vye_v):
                sy.wait_ge(v_ye.h, vye_v)
                ag3w.tag(sy.dma_start(out=ag3i[b][:, CH * c8:CH * (c8 + 1)],
                                      in_=ybf_r[yslot][:]), 16)
            op("sync", wr_y)
            for _ in range(fu_per_chunk):
                try:
                    ops.append(next(fu_iter))
                except StopIteration:
                    break
        if SUB >= 3:
            for rest in fu_iter:
                ops.append(rest)
        if SUB < 2 or SUBY < 3:
            continue
        ag3_v = ag3w.count
        cc3.reserve(1)
        def coll3(g, b=b, ag3_v=ag3_v, wld_all=w_ld.count, sp_all=wt_sp.count):
            g.wait_ge(ag3w.h, ag3_v)
            if b == 0:
                g.wait_ge(cc2.h, B)
                g.wait_ge(cc1.h, B)
                g.wait_ge(w_ld.h, wld_all)
                g.wait_ge(wt_sp.h, sp_all)
            else:
                g.wait_ge(cc3.h, b)
            cc3.tag(g.collective_compute("AllGather", ALU.bypass, replica_groups=RG,
                                         ins=[ag3i[b]], outs=[ag3o[b]]))
        op("gpsimd", coll3)

    if STAGE < 3:
        def fin_stub(g):
            g.wait_ge(cc2.h, cc2.count)
            g.wait_ge(cc1.h, cc1.count)
            if STAGE >= 2:
                g.wait_ge(cc3.h, cc3.count)
        op("gpsimd", fin_stub)
    c_final = c_a if B % 2 == 0 else c_b
    if STAGE >= 3:
        TG_FINAL = t_gate.count
        wo_v = wg_ld.reserve(32)
        def ld_wo(sc):
            sc.wait_ge(t_gate.h, TG_FINAL)
            sc.wait_ge(wt_sp.h, ALL_SPILLS)
            wg_ld.tag(sc.dma_start(out=wg_w[:], in_=wts[6].rearrange("(k p) c -> p k c", p=128)), 16)
            wg_ld.tag(sc.dma_start(out=wg_u[:], in_=wts[7].rearrange("(k p) c -> p k c", p=128)), 16)
        op("scalar", ld_wo)
        VC_FINAL = v_c.count
        for b in range(B):
            two_guard = t_wo.count
            xn3_v = sc_ld.reserve(16)
            def ld_xn3(sc, b=b, two_guard=two_guard):
                if two_guard > 0:
                    sc.wait_ge(t_wo.h, two_guard)
                else:
                    sc.wait_ge(t_gate.h, TG_FINAL)
                sc_ld.tag(sc.dma_start(out=xnT[:], in_=xnts[b].rearrange("(k p) t -> p k t", p=128)), 16)
            op("scalar", ld_xn3)
            vt1_guard = v_t1.count
            gyb_v = g_yb.reserve(16)
            def ld_yb(sy, b=b, vt1_guard=vt1_guard):
                sy.wait_ge(cc3.h, b + 1)
                if vt1_guard > 0:
                    sy.wait_ge(v_t1.h, vt1_guard)
                g_yb.tag(sy.dma_start(out=ybT[:],
                                      in_=ag3o[b].rearrange("(k p) t -> p k t", p=128)[:, :, dynslice(sy, TK)]), 16)
            op("sync", ld_yb)
            for m in range(KB):
                for h2 in range(2):
                    j = m * 2 + h2
                    sh_guard = s_h.count - 1 if s_h.count >= 1 else -1
                    two_v = t_wo.reserve()
                    def mm_wo(t, m=m, h2=h2, sh_guard=sh_guard, c_final=c_final,
                              xn3_v=xn3_v):
                        t.wait_ge(wg_ld.h, wo_v)
                        t.wait_ge(sc_ld.h, xn3_v)
                        t.wait_ge(v_c.h, VC_FINAL)
                        if sh_guard >= 1:
                            t.wait_ge(s_h.h, sh_guard)
                        sl = slice(512 * h2, 512 * (h2 + 1))
                        for k in range(KB):
                            t.matmul(pgate[:, sl], wg_w[:, k, 128 * m:128 * (m + 1)],
                                     xnT[:, k, sl], start=(k == 0), stop=False)
                        for k in range(KB):
                            mm = t.matmul(pgate[:, sl], wg_u[:, k, 128 * m:128 * (m + 1)],
                                          c_final[:, k, sl], start=False, stop=(k == KB - 1))
                        t_wo.tag(mm)
                    op("tensor", mm_wo)
                    hslot = j % 2
                    vt1_g = v_t1.count - 1 if v_t1.count >= 2 else -1
                    sh_v = s_h.reserve()
                    def mk_h(s, m=m, h2=h2, hslot=hslot, two_v=two_v, vt1_g=vt1_g):
                        s.wait_ge(t_wo.h, two_v)
                        if vt1_g >= 0:
                            s.wait_ge(v_t1.h, vt1_g)
                        sl = slice(512 * h2, 512 * (h2 + 1))
                        s_h.tag(s.activation(hch[hslot][:], pgate[:, sl], AF.Tanh,
                                             bias=bias_t[:, m, 3:4]))
                    op("scalar", mk_h)
                    ttr_guard = t_tr.count
                    v_t1.reserve()
                    def mk_t1(v, m=m, h2=h2, j=j, hslot=hslot, sh_v=sh_v, gyb_v=gyb_v,
                              ttr_guard=ttr_guard, b=b):
                        v.wait_ge(s_h.h, sh_v)
                        v.wait_ge(g_yb.h, gyb_v)
                        if b > 0 and j == 0 and ttr_guard > 0:
                            v.wait_ge(t_tr.h, ttr_guard)
                        sl = slice(512 * h2, 512 * (h2 + 1))
                        v_t1.tag(v.tensor_add(t1[:, j, :], hch[hslot][:], ybT[:, m, sl]))
                    op("vector", mk_t1)
            vt1_v = v_t1.count
            for q in range(KB):
                jq = b * KB + q
                slot = q % 2
                vout_g = v_out.count - 1 if jq >= 2 else -1
                x2_v = x_ld.reserve(16)
                def ldx2(sy, b=b, q=q, slot=slot, vout_g=vout_g):
                    if vout_g >= 0:
                        sy.wait_ge(v_out.h, vout_g + 1)
                    x_ld.tag(sy.dma_start(out=xch2[slot][:], in_=x_in[b, 128 * q:128 * (q + 1), :]), 16)
                op("sync", ldx2)
                vo_serial = v_out.count
                ttr_v = t_tr.reserve()
                def tr_t1(t, q=q, vt1_v=vt1_v, vo_serial=vo_serial):
                    t.wait_ge(v_t1.h, vt1_v)
                    if vo_serial > 0:
                        t.wait_ge(v_out.h, vo_serial)
                    h2 = q // 4
                    qq = q % 4
                    for m in range(KB):
                        mm = t.transpose(pdt[:, 128 * m:128 * (m + 1)],
                                         t1[:, m * 2 + h2, 128 * qq:128 * (qq + 1)], eye_f[:])
                    t_tr.tag(mm)
                op("tensor", tr_t1)
                sout_g = st_out.count - 16 if jq >= 2 else -1
                vo_v = v_out.reserve()
                def mk_out(v, slot=slot, ttr_v=ttr_v, x2_v=x2_v, sout_g=sout_g):
                    v.wait_ge(t_tr.h, ttr_v)
                    v.wait_ge(x_ld.h, x2_v)
                    if sout_g >= 0:
                        v.wait_ge(st_out.h, sout_g + 16)
                    v_out.tag(v.tensor_add(outch[slot][:], xch2[slot][:], pdt[:]))
                op("vector", mk_out)
                st_out.reserve(16)
                def st_o(sy, b=b, q=q, slot=slot, vo_v=vo_v):
                    sy.wait_ge(v_out.h, vo_v)
                    st_out.tag(sy.dma_start(out=out_t[b, 128 * q:128 * (q + 1), :],
                                            in_=outch[slot][:]), 16)
                op("sync", st_o)
        FIN = st_out.count
        def fin_wait(sy):
            sy.wait_ge(st_out.h, FIN)
        op("sync", fin_wait)

    if DUMP:
        dmp1 = nc.dram_tensor("dump_ag1", [96 * NC, TK], BF16, kind="ExternalOutput").ap()
        dmp2 = nc.dram_tensor("dump_ag2", [D * NC, TK], BF16, kind="ExternalOutput").ap()
        dmp3 = nc.dram_tensor("dump_ag3", [DK * NC, L], BF16, kind="ExternalOutput").ap()
        dmp3b = nc.dram_tensor("dump_ag3b", [3, DK * NC, L], BF16, kind="ExternalOutput").ap()
        dmp1b = nc.dram_tensor("dump_ag1b", [3, 96 * NC, TK], BF16, kind="ExternalOutput").ap()
        dmp2b = nc.dram_tensor("dump_ag2b", [3, D * NC, TK], BF16, kind="ExternalOutput").ap()
        dmp2i = nc.dram_tensor("dump_ag2i", [B, D, TK], BF16, kind="ExternalOutput").ap()
        dmpxt = nc.dram_tensor("dump_xnts", [B, D, TK], BF16, kind="ExternalOutput").ap()
        dval = st_out.count
        st_out.reserve(320)
        def dump(g):
            g.wait_ge(st_out.h, dval)
            g.wait_ge(cc1.h, cc1.count)
            g.wait_ge(cc2.h, cc2.count)
            g.wait_ge(cc3.h, cc3.count)
            st_out.tag(g.dma_start(out=dmp1, in_=ag1o[0]), 16)
            st_out.tag(g.dma_start(out=dmp2, in_=ag2o[0]), 16)
            st_out.tag(g.dma_start(out=dmp3, in_=ag3o[0]), 16)
            for _bb in range(1, 4):
                st_out.tag(g.dma_start(out=dmp3b[_bb - 1], in_=ag3o[_bb]), 16)
                st_out.tag(g.dma_start(out=dmp1b[_bb - 1], in_=ag1o[_bb]), 16)
                st_out.tag(g.dma_start(out=dmp2b[_bb - 1], in_=ag2o[_bb]), 16)
            for _bb in range(B):
                st_out.tag(g.dma_start(out=dmp2i[_bb], in_=ag2i[_bb]), 16)
                st_out.tag(g.dma_start(out=dmpxt[_bb], in_=xnts[_bb]), 16)
        op("gpsimd", dump)
        DFIN = st_out.count
        def dump_wait(sy):
            sy.wait_ge(st_out.h, DFIN)
        op("sync", dump_wait)

    block = ctx.enter_context(nc.Block())

    @block.sync
    def _(sync):
        for eng, fn in ops:
            if eng == "sync":
                fn(sync)

    @block.gpsimd
    def _(gpsimd):
        for eng, fn in ops:
            if eng == "gpsimd":
                fn(gpsimd)

    @block.vector
    def _(vector):
        for eng, fn in ops:
            if eng == "vector":
                fn(vector)

    @block.scalar
    def _(scalar):
        for eng, fn in ops:
            if eng == "scalar":
                fn(scalar)

    @block.tensor
    def _(tensor):
        for eng, fn in ops:
            if eng == "tensor":
                fn(tensor)

    ctx.close()
    return nc


_NC_CACHE = {}


def kernel(L0_attn_x, L0_attn_W_xproj, L0_attn_W_dt, L0_attn_b_dt, L0_attn_A_log,
           L0_attn_D_param, L0_ffn_wf, L0_ffn_uf, L0_ffn_bf, L0_ffn_wi, L0_ffn_ui,
           L0_ffn_bi, L0_ffn_wc, L0_ffn_uc, L0_ffn_bc, L0_ffn_wo, L0_ffn_uo,
           L0_ffn_bo, L0_ffn_in_c):
    if "nc" not in _NC_CACHE:
        _NC_CACHE["nc"] = build_program()
    nc = _NC_CACHE["nc"]

    x = np.ascontiguousarray(np.asarray(L0_attn_x, dtype=np.float32))
    eye_f = np.eye(128, dtype=np.float32)
    eye_b = np.eye(128, dtype=ml_dtypes.bfloat16)
    sel = np.zeros((N, N, 128), dtype=ml_dtypes.bfloat16)
    for n in range(N):
        sel[n, n, :] = 1
    sel = np.ascontiguousarray(sel.reshape(N, N * 128))
    bias_pack = np.stack([np.asarray(v, dtype=np.float32).reshape(KB, 128).T
                          for v in (L0_ffn_bf, L0_ffn_bi, L0_ffn_bc, L0_ffn_bo, L0_ffn_in_c)],
                         axis=2)
    bias_pack = np.ascontiguousarray(bias_pack)
    wbig = [np.ascontiguousarray(np.asarray(w, dtype=np.float32).T) for w in
            (L0_ffn_wf, L0_ffn_uf, L0_ffn_wi, L0_ffn_ui, L0_ffn_wc, L0_ffn_uc,
             L0_ffn_wo, L0_ffn_uo)]
    wx = np.ascontiguousarray(np.asarray(L0_attn_W_xproj, dtype=np.float32).T)

    in_maps = []
    for c in range(NC):
        dsl = slice(DK * c, DK * (c + 1))
        m = {
            "x_in": np.ascontiguousarray(x[:, TK * c:TK * (c + 1), :]),
            "wx_in": wx,
            "wdt_in": np.ascontiguousarray(np.asarray(L0_attn_W_dt, dtype=np.float32)[dsl, :].T),
            "bdt_in": np.ascontiguousarray(np.asarray(L0_attn_b_dt, dtype=np.float32)[dsl].reshape(DK, 1)),
            "alog_in": np.ascontiguousarray(np.asarray(L0_attn_A_log, dtype=np.float32)[dsl, :]),
            "dvec_in": np.ascontiguousarray(np.asarray(L0_attn_D_param, dtype=np.float32)[dsl].reshape(DK, 1)),
            "bias_in": bias_pack,
            "eye_f_in": eye_f,
            "eye_b_in": eye_b,
            "sel_in": sel,
        }
        for g in range(8):
            m[f"w{g}_in"] = wbig[g]
        in_maps.append(m)

    res = run_bass_kernel_spmd(nc, in_maps, core_ids=list(range(NC)))
    return np.concatenate([np.asarray(r["out"]) for r in res.results], axis=1)
